# version 14 (final)
# baseline (speedup 1.0000x reference)
import contextlib
import numpy as np
import concourse.bass as bass
import concourse.mybir as mybir
from concourse.bass_utils import run_bass_kernel_spmd

F32 = mybir.dt.float32
BF16 = mybir.dt.bfloat16
I32 = mybir.dt.int32
Alu = mybir.AluOpType
Act = mybir.ActivationFunctionType
AX = mybir.AxisListType

NCORE = 8
DEBUG_DUMPS = False
TRUNC = 99
T4TE = True
T4AB = True
T4G = True
T4GP = True
T4GV = True
T4G_G = True
T4G_M = True
T4G_D = True
RACECHECK = False
D = 1024
N = 16
TC = 2048
NT = 16
KC = 8
MAGIC = 12582912.0
RD = float(np.float32(1.0 / 1024.0))


def build_nc():
    nc = bass.Bass("TRN2", target_bir_lowering=False,
                   detect_race_conditions=RACECHECK)

    def din(name, shape, dt=F32):
        return nc.dram_tensor(name, shape, dt, kind="ExternalInput").ap()

    x_d = din("x", [TC, D])
    wgT_d = din("wgT", [D, D])
    wdT_d = din("wdT", [D, D])
    wfT_d = din("wfT", [D, D])
    wabT_d = din("wabT", [D, 3 * N])
    wcT_d = din("wcT", [N, D])
    gamma_d = din("gamma", [1, D])
    beta_d = din("beta", [1, D])
    bD_d = din("bD", [1, D])
    bg_d = din("bg", [1, D])
    myoff_d = din("myoff", [1, 1], I32)
    partoff_d = din("partoff", [1, 1], I32)
    out_d = nc.dram_tensor("out", [TC, D], F32, kind="ExternalOutput").ap()
    if DEBUG_DUMPS:
        dbgA_d = nc.dram_tensor("dbgA", [TC, D], F32, kind="ExternalOutput").ap()
        dbgB_d = nc.dram_tensor("dbgB", [TC, D], F32, kind="ExternalOutput").ap()
        dbgS_d = nc.dram_tensor("dbgS", [1, 16], F32, kind="ExternalOutput").ap()
        dbgM_d = nc.dram_tensor("dbgM", [128, KC * D], BF16,
                                kind="ExternalOutput").ap()
        dbgQ_d = nc.dram_tensor("dbgQ", [128, NT * KC * 128], BF16,
                                kind="ExternalOutput").ap()

    xn2_d = nc.dram_tensor("xn2d", [TC, D], F32).ap()
    cc1_in = nc.dram_tensor("cc1_in", [8, 40], F32)
    cc1_out = nc.dram_tensor("cc1_out", [8, 40], F32)
    cc2_in = nc.dram_tensor("cc2_in", [8, 4], F32)
    cc2_out = nc.dram_tensor("cc2_out", [8, 4], F32)
    scrA = nc.dram_tensor("scrA", [128], F32)
    scrH = nc.dram_tensor("scrH", [16], F32)
    scrH2 = nc.dram_tensor("scrH2", [16], F32)
    scrB = nc.dram_tensor("scrB", [8], F32)

    S = lambda *a, **k: nc.alloc_sbuf_tensor(*a, **k)
    xbuf = S("xbuf", [128, NT, D], F32)
    xnTq = S("xnTq", [128, NT, KC, 128], BF16, is_transpose_dest=True)
    wgTb = S("wgTb", [128, KC, D], BF16)
    wdTb = S("wdTb", [128, KC, D], BF16)
    mTb = S("mTb", [128, KC, D], BF16)
    wcTb = S("wcTb", [N, D], BF16)
    wchb = S("wchb", [N, D], BF16)
    wabTb = S("wabTb", [128, KC, 3 * N], BF16)
    onesb = S("onesb", [1, 128], BF16)
    bDb = S("bDb", [1, D], BF16)
    bgb = S("bgb", [1, D], BF16)
    gammaB = S("gammaB", [128, D], F32)
    betaB = S("betaB", [128, D], F32)
    xn2r = S("xn2r", [128, 2, D], F32)
    xnbq = S("xnbq", [128, 2, D], BF16)
    gmr = S("gmr", [128, 2, D], BF16)
    dxe = S("dxe", [128, 2, D], BF16)
    wfr = S("wfr", [128, 2, D], F32)
    sgn2 = S("sgn2", [128, 2, D], BF16)
    a_bf = S("a_bf", [N, TC], BF16)
    b_bf = S("b_bf", [N, TC], BF16)
    hsb = a_bf
    Pb = b_bf
    hscr = S("hscr", [N, 512], F32)
    pscr = S("pscr", [N, 512], F32)
    sumc = S("sumc", [128, NT], F32)
    ssc = S("ssc", [128, NT], F32)
    muc = S("muc", [128, NT], F32)
    invc = S("invc", [128, NT], F32)
    c2c = S("c2c", [128, NT], F32)
    nm1c = S("nm1c", [128, NT], F32)
    nm2c = S("nm2c", [128, NT], F32)
    tmpc = S("tmpc", [128, NT], F32)
    tmc2 = S("tmc2", [128, NT], F32)
    lmax = S("lmax", [128, NT], F32)
    lmax1 = S("lmax1", [128, 1], F32)
    awc = S("awc", [128, KC], F32)
    nmc = S("nmc", [128, KC], F32)
    sqc = S("sqc", [128, KC], F32)
    class _ColView:
        def __init__(self, t, j):
            self._t, self._j = t, j

        def ap(self):
            return self._t.ap()[:, self._j:self._j + 1]

    colp = S("colp", [128, 8], F32)
    col1 = _ColView(colp, 0)
    deltaP = _ColView(colp, 1)
    deltaN = _ColView(colp, 2)
    sinvc = _ColView(colp, 3)
    cbc = _ColView(colp, 4)
    magicc = _ColView(colp, 5)
    epsc = _ColView(colp, 6)
    row128 = S("row128", [1, 128], F32)
    z840 = S("z840", [8, 40], F32)
    gath2 = S("gath2", [8, 4], F32)
    hinc = S("hinc", [N, 1], F32)
    sc11 = S("sc11", [1, 12], F32)
    myoff_s = S("myoff_s", [1, 1], I32)
    partoff_s = S("partoff_s", [1, 1], I32)

    zgA = nc.alloc_psum_tensor("zgA", [128, D], F32)
    zgB = nc.alloc_psum_tensor("zgB", [128, D], F32)
    dxP = nc.alloc_psum_tensor("dxP", [128, D], F32)
    uP = nc.alloc_psum_tensor("uP", [128, 512], F32)
    abP = nc.alloc_psum_tensor("abP", [3 * N, 512], F32)


    def gview(tt):
        return xnTq.ap()[:, tt, :, :].rearrange("p k f -> p (k f)")

    ctx = contextlib.ExitStack()
    sem = {}
    for name in (["s_gb", "s_w", "s_ssq", "s_std", "s_sts", "s_std2",
                  "s_sts2", "s_stats", "s_xnb", "s_z",
                  "s_xn2", "s_lm", "s_ab", "s_abe", "s_scan",
                  "s_hsb", "s_zg", "s_dx", "s_u", "s_g", "s_gm", "s_p2",
                  "s_s1", "s_p1", "s_s2", "s_cc1", "s_wch", "s_cc2", "s_q",
                  "s_ffn", "s_out", "s_st", "s_mz",
                  "s_aw", "s_dcol", "s_b1", "s_b2", "s_b3", "s_b4",
                  "s_b5", "s_b6", "s_delta", "s_sg", "s_m",
                  "s_num", "s_sq2", "s_acol", "s_alpha", "s_gm2", "s_sv1",
                  "s_sinv", "s_cb", "s_io", "s_dscr", "s_c2s", "s_c2b",
                  "s_svs", "s_cbs", "s_io2", "s_sum", "s_dxe", "s_dbgA",
                  "s_dbgB"]
                 + ["s_wf1", "s_wf2"]
                 + [f"s_xg{i}" for i in range(4)]
                 + [f"s_xTp{i}" for i in range(NT // 2)]
                 + [f"s_qTp{i}" for i in range(NT // 2)]
                 + [f"s_spk{i}" for i in range(KC)]
                 + [f"s_rlk{i}" for i in range(KC)]):
        sem[name] = ctx.enter_context(nc.semaphore(name))

    with ctx:
        with nc.Block() as blk:

            @blk.sync
            def _(sync):
                if TRUNC >= 2:
                    for tt in range(NT):
                        sync.wait_ge(sem["s_xnb"], tt + 1)
                        sync.dma_start(
                            out=xnTq.ap()[:, tt, :, :],
                            in_=xnbq.ap()[:, tt % 2, :].rearrange(
                                "p (k f) -> p k f", k=KC),
                            transpose=True).then_inc(sem[f"s_xTp{tt // 2}"], 16)
                if TRUNC >= 3:
                    for k in range(KC):
                        if k == 4:
                            sync.wait_ge(sem["s_dscr"], 16)
                            sync.dma_start(out=deltaP.ap(),
                                           in_=scrB.ap()[4:5].rearrange(
                                               "(a b) -> a b", a=1)
                                           .broadcast_to([128, 1])).then_inc(sem["s_delta"], 16)
                            sync.dma_start(out=deltaN.ap(),
                                           in_=scrB.ap()[5:6].rearrange(
                                               "(a b) -> a b", a=1)
                                           .broadcast_to([128, 1])).then_inc(sem["s_delta"], 16)
                        sync.wait_ge(sem["s_xn2"], 2 * k + 2)
                        sl2 = slice(0, 2)
                        sync.dma_start(
                            out=xn2_d[k * 256:(k + 1) * 256, :].rearrange(
                                "(s p) d -> p s d", s=2),
                            in_=xn2r.ap()[:, sl2, :]).then_inc(sem[f"s_spk{k}"], 16)
                if TRUNC >= 6:
                    sync.wait_ge(sem["s_c2s"], 16)
                    sync.dma_start(out=cc2_in.ap(),
                                   in_=z840.ap()[0:8, 0:4]).then_inc(sem["s_io2"], 16)
                    sync.wait_ge(sem["s_io2"], 16)
                    sync.dma_start(out=cc2_in.ap()[:, 0:1],
                                   in_=scrB.ap()[0:1].rearrange("(a b) -> a b", a=1)
                                   .broadcast_to([8, 1]),
                                   allow_slow_non_contiguous=True).then_inc(sem["s_c2b"], 16)
                if DEBUG_DUMPS:
                    for tt in range(NT):
                        sync.wait_ge(sem["s_s1"], NT)
                        sync.dma_start(out=dbgA_d[tt * 128:(tt + 1) * 128, :],
                                       in_=xbuf.ap()[:, tt, :]).then_inc(sem["s_dbgA"], 16)
                if TRUNC >= 6:
                    sync.wait_ge(sem["s_svs"], 16)
                    sync.dma_start(out=sinvc.ap(),
                                   in_=scrB.ap()[1:2].rearrange(
                                       "(a b) -> a b", a=1)
                                   .broadcast_to([128, 1])).then_inc(sem["s_sinv"], 16)
                for tt in range(NT if TRUNC >= 6 else 0):
                    sync.wait_ge(sem["s_q"], tt + 1)
                    sync.wait_ge(sem["s_p1"], 2 * tt + 2)
                    sync.wait_ge(sem["s_s2"], tt + 1)
                    sync.dma_start(
                        out=xnTq.ap()[:, tt, :, :],
                        in_=xnbq.ap()[:, tt % 2, :].rearrange(
                            "p (k f) -> p k f", k=KC),
                        transpose=True).then_inc(sem[f"s_qTp{tt // 2}"], 16)
                if DEBUG_DUMPS:
                    for tt in range(NT):
                        sync.wait_ge(sem["s_s2"], NT)
                        sync.dma_start(out=dbgB_d[tt * 128:(tt + 1) * 128, :],
                                       in_=xbuf.ap()[:, tt, :]).then_inc(sem["s_dbgB"], 16)
                if TRUNC >= 6:
                    sync.wait_ge(sem["s_cbs"], 16)
                    sync.dma_start(out=cbc.ap(),
                                   in_=scrB.ap()[2:3].rearrange("(a b) -> a b", a=1)
                                   .broadcast_to([128, 1])).then_inc(sem["s_cb"], 16)
                if DEBUG_DUMPS:
                    sync.wait_ge(sem["s_cb"], 16)
                    sync.wait_ge(sem["s_sinv"], 16)
                    sync.wait_ge(sem["s_delta"], 32)
                    sync.dma_start(out=dbgS_d[0:1, 0:12],
                                   in_=sc11.ap()).then_inc(sem["s_io2"], 16)
                    sync.dma_start(out=dbgS_d[0:1, 12:13],
                                   in_=deltaP.ap()[0:1, :]).then_inc(sem["s_io2"], 16)
                    sync.dma_start(out=dbgS_d[0:1, 13:14],
                                   in_=sinvc.ap()[0:1, :]).then_inc(sem["s_io2"], 16)
                    sync.dma_start(out=dbgS_d[0:1, 14:15],
                                   in_=cbc.ap()[0:1, :]).then_inc(sem["s_io2"], 16)
                    sync.dma_start(out=dbgS_d[0:1, 15:16],
                                   in_=gath2.ap()[0:1, 0:1]).then_inc(sem["s_io2"], 16)
                    sync.wait_ge(sem["s_io2"], 96)
                for tt in range(NT):
                    if TRUNC >= 6:
                        sync.wait_ge(sem["s_out"], tt + 1)
                    elif TRUNC == 5:
                        sync.wait_ge(sem["s_s2"], tt + 1)
                    elif TRUNC == 4 and T4GP:
                        sync.wait_ge(sem["s_s1"], tt + 1)
                    elif TRUNC == 4:
                        sync.wait_ge(sem["s_dx"], tt + 1)
                    elif TRUNC == 3:
                        sync.wait_ge(sem["s_xn2"], tt + 1)
                    elif TRUNC == 2:
                        sync.wait_ge(sem[f"s_xTp{tt // 2}"], 32)
                    elif TRUNC == 1:
                        sync.wait_ge(sem["s_xnb"], tt + 1)
                    else:
                        sync.wait_ge(sem[f"s_xg{tt // 4}"], 64)
                    sync.dma_start(out=out_d[tt * 128:(tt + 1) * 128, :],
                                   in_=xbuf.ap()[:, tt, :]).then_inc(sem["s_st"], 16)
                sync.wait_ge(sem["s_st"], 16 * NT)
                if DEBUG_DUMPS:
                    sync.dma_start(out=dbgM_d,
                                   in_=mTb.ap().rearrange(
                                       "p k d -> p (k d)")).then_inc(sem["s_io2"], 16)
                    sync.dma_start(out=dbgQ_d,
                                   in_=xnTq.ap().rearrange(
                                       "p a b c -> p (a b c)")).then_inc(sem["s_io2"], 16)
                    sync.wait_ge(sem["s_io2"], 128)

            @blk.scalar
            def _(sc):
                sc.dma_start(out=gammaB.ap(),
                             in_=gamma_d.broadcast_to([128, D])).then_inc(sem["s_gb"], 16)
                sc.dma_start(out=betaB.ap(),
                             in_=beta_d.broadcast_to([128, D])).then_inc(sem["s_gb"], 16)
                for tt in range(NT):
                    sc.dma_start(out=xbuf.ap()[:, tt, :],
                                 in_=x_d[tt * 128:(tt + 1) * 128, :]).then_inc(
                                     sem[f"s_xg{tt // 4}"], 16)
                for g in range(4 if TRUNC >= 1 else 0):
                    sc.wait_ge(sem[f"s_xg{g}"], 64)
                    for t in range(4 * g, 4 * g + 4):
                        if t >= 1:
                            sc.wait_ge(sem["s_ssq"], t)
                        if g >= 1:
                            sc.wait_ge(sem["s_aw"], 2 * g)
                        sc.activation(xn2r.ap()[:, 0, :], xbuf.ap()[:, t, :],
                                      Act.Square, bias=0.0, scale=1.0,
                                      accum_out=ssc.ap()[:, t:t + 1]).then_inc(sem["s_ssq"], 1)
                    sl = slice(4 * g, 4 * g + 4)
                    sc.wait_ge(sem["s_std"], g + 1)
                    sc.activation(invc.ap()[:, sl], tmpc.ap()[:, sl],
                                  Act.Sqrt, bias=epsc.ap(), scale=1.0).then_inc(sem["s_sts"], 1)
                    sc.wait_ge(sem["s_std2"], g + 1)
                    sc.activation(c2c.ap()[:, sl], tmc2.ap()[:, sl],
                                  Act.Sqrt, bias=epsc.ap(), scale=1.0).then_inc(sem["s_sts2"], 1)
                    for k in (2 * g, 2 * g + 1):
                        sc.wait_ge(sem["s_wf1"], 16 * (k + 1))
                        sc.wait_ge(sem["s_ssq"], 4 * g + 4)
                        sc.activation(xn2r.ap()[:, 1, :], wfr.ap()[:, 0, :],
                                      Act.Abs, bias=0.0, scale=1.0,
                                      accum_out=awc.ap()[:, k:k + 1]).then_inc(sem["s_aw"], 1)
                for tt in range(NT + 11 if TRUNC >= 3 else 0):
                    zt = tt
                    if 0 <= zt < NT:
                        sc.wait_ge(sem["s_stats"], zt // 4 + 1)
                        if zt >= 2:
                            sc.wait_ge(sem[f"s_spk{(zt - 2) // 2}"], 16)
                            sc.wait_ge(sem["s_lm"], zt - 1)
                        sc.activation(xn2r.ap()[:, zt % 2, :], xbuf.ap()[:, zt, :],
                                      Act.Identity, bias=nm2c.ap()[:, zt:zt + 1],
                                      scale=c2c.ap()[:, zt:zt + 1]).then_inc(sem["s_z"], 1)
                    et = tt - 2
                    if 0 <= et < NT and TRUNC >= 4 and T4G:
                        par = [zgA, zgB][et % 2]
                        sc.wait_ge(sem["s_dx"], et + 1)
                        if et >= 2 and T4GP:
                            sc.wait_ge(sem["s_s1"], et - 1)
                        if T4G_M:
                            sc.activation(gmr.ap()[:, et % 2, :], par.ap(),
                                          Act.Sigmoid, bias=0.0,
                                          scale=-1.0).then_inc(sem["s_gm"], 1)
                        if T4G_D:
                            sc.copy(dxe.ap()[:, et % 2, 0:512],
                                    dxP.ap()[:, 0:512])
                            sc.copy(dxe.ap()[:, et % 2, 512:1024],
                                    dxP.ap()[:, 512:1024]).then_inc(sem["s_dxe"], 1)
                        if T4G_G:
                            if et % 4 <= 1:
                                sc.wait_ge(sem["s_ab"], et // 4 + 1)
                            gdst = gview(et) if T4GV else xnbq.ap()[:, et % 2, :]
                            sc.activation(gdst, par.ap(), Act.Sigmoid,
                                          bias=0.0, scale=1.0).then_inc(sem["s_g"], 1)
                    if tt % 4 == 0 and 1 <= tt // 4 <= 4 and TRUNC >= 4 and T4AB:
                        g0 = tt // 4 - 1
                        sc.wait_ge(sem["s_ab"], g0 + 1)
                        sc.activation(a_bf.ap()[:, g0 * 512:(g0 + 1) * 512],
                                      abP.ap()[0:N, :], Act.Tanh, bias=0.0,
                                      scale=1.0)
                        sc.copy(b_bf.ap()[:, g0 * 512:(g0 + 1) * 512],
                                abP.ap()[2 * N:3 * N, :]).then_inc(sem["s_abe"], 1)
                    if tt % 4 == 2 and 1 <= (tt - 2) // 4 <= 4 and TRUNC >= 4 and T4AB:
                        g1 = (tt - 2) // 4 - 1
                        sc.wait_ge(sem["s_scan"], g1 + 1)
                        sc.copy(hsb.ap()[:, g1 * 512:(g1 + 1) * 512], hscr.ap())
                        sc.copy(Pb.ap()[:, g1 * 512:(g1 + 1) * 512],
                                pscr.ap()).then_inc(sem["s_hsb"], 1)
                    k = tt - 13
                    if 0 <= k < KC and TRUNC >= 6:
                        wfk = wfr.ap()[:, k % 2, :]
                        if k % 2 == 0:
                            sc.wait_ge(sem["s_wf2"], 16 * (k // 2 + 1))
                        else:
                            sc.wait_ge(sem["s_wf1"], 128 + 16 * (k // 2 + 1))
                        sc.wait_ge(sem["s_delta"], 32)
                        if k >= 1:
                            sc.wait_ge(sem["s_m"], k)
                        sc.activation(sgn2.ap()[:, 0, :], wfk,
                                      Act.Sign, bias=deltaN.ap(), scale=1.0)
                        sc.activation(sgn2.ap()[:, 1, :], wfk,
                                      Act.Sign, bias=deltaP.ap(), scale=1.0)
                        sc.activation(mTb.ap()[:, k, :], wfk, Act.Relu,
                                      bias=deltaN.ap(), scale=1.0,
                                      accum_out=nmc.ap()[:, k:k + 1])
                        sc.activation(mTb.ap()[:, k, :], wfk, Act.Relu,
                                      bias=deltaN.ap(), scale=-1.0,
                                      accum_out=sqc.ap()[:, k:k + 1]).then_inc(sem["s_sg"], 1)
                    k2 = tt - 15
                    if 0 <= k2 < KC and TRUNC >= 6:
                        if k2 == 0:
                            sc.wait_ge(sem[f"s_xTp{7}"], 32)
                        sc.wait_ge(sem["s_m"], k2 + 1)
                        sc.activation(xnbq.ap()[:, 0, :], mTb.ap()[:, k2, :],
                                      Act.Abs, bias=0.0, scale=0.5,
                                      accum_out=awc.ap()[:, k2:k2 + 1]).then_inc(sem["s_sq2"], 1)



            @blk.vector
            def _(v):
                mzc = [1]

                def mzp(instr):
                    instr.then_inc(sem["s_mz"], 1)
                    mzc[0] += 1

                def mzf():
                    v.wait_ge(sem["s_mz"], mzc[0])

                v.memset(magicc.ap(), MAGIC)
                v.memset(epsc.ap(), 1e-5)
                v.memset(z840.ap(), 0.0)
                v.memset(onesb.ap(), 1.0).then_inc(sem["s_mz"], 1)
                for g in range(4 if TRUNC >= 1 else 0):
                    v.wait_ge(sem[f"s_xg{g}"], 64)
                    for t in range(4 * g, 4 * g + 4):
                        ins = v.reduce_sum(sumc.ap()[:, t:t + 1],
                                           xbuf.ap()[:, t, :],
                                           AX.X)
                        if t == 4 * g + 3:
                            mzp(ins)
                    sl = slice(4 * g, 4 * g + 4)
                    v.wait_ge(sem["s_ssq"], 4 * g + 4)
                    mzf()
                    mzp(v.tensor_scalar(muc.ap()[:, sl], sumc.ap()[:, sl], RD,
                                        None, Alu.mult))
                    v.tensor_scalar(tmpc.ap()[:, sl], ssc.ap()[:, sl], RD,
                                    None, Alu.mult)
                    mzf()
                    mzp(v.tensor_tensor(tmc2.ap()[:, sl], muc.ap()[:, sl],
                                        muc.ap()[:, sl], Alu.mult))
                    mzf()
                    v.tensor_tensor(tmpc.ap()[:, sl], tmpc.ap()[:, sl],
                                    tmc2.ap()[:, sl], Alu.subtract).then_inc(sem["s_std"], 1)
                    v.wait_ge(sem["s_sts"], g + 1)
                    mzp(v.reciprocal(invc.ap()[:, sl], invc.ap()[:, sl]))
                    mzf()
                    mzp(v.tensor_tensor(tmc2.ap()[:, sl], tmpc.ap()[:, sl],
                                        invc.ap()[:, sl], Alu.mult))
                    mzf()
                    v.tensor_tensor(tmc2.ap()[:, sl], tmc2.ap()[:, sl],
                                    invc.ap()[:, sl], Alu.mult).then_inc(sem["s_std2"], 1)
                    v.wait_ge(sem["s_sts2"], g + 1)
                    mzp(v.reciprocal(c2c.ap()[:, sl], c2c.ap()[:, sl]))
                    mzf()
                    mzp(v.tensor_tensor(c2c.ap()[:, sl], invc.ap()[:, sl],
                                        c2c.ap()[:, sl], Alu.mult))
                    v.scalar_tensor_tensor(nm1c.ap()[:, sl], muc.ap()[:, sl],
                                           -1.0, invc.ap()[:, sl], Alu.mult,
                                           Alu.mult)
                    mzf()
                    mzp(v.scalar_tensor_tensor(nm2c.ap()[:, sl],
                                               muc.ap()[:, sl],
                                               -1.0, c2c.ap()[:, sl], Alu.mult,
                                               Alu.mult))
                    mzf()
                    v.sem_inc(sem["s_stats"], 1)
                    for t in range(4 * g, 4 * g + 4):
                        if t >= 2 and TRUNC >= 2:
                            v.wait_ge(sem[f"s_xTp{(t - 2) // 2}"], 32)
                        v.tensor_scalar(xnbq.ap()[:, t % 2, :],
                                        xbuf.ap()[:, t, :],
                                        invc.ap()[:, t:t + 1],
                                        nm1c.ap()[:, t:t + 1], Alu.mult,
                                        Alu.add).then_inc(sem["s_xnb"], 1)
                if TRUNC >= 3:
                    v.wait_ge(sem["s_aw"], KC)
                    v.reduce_sum(col1.ap(), awc.ap(), AX.X).then_inc(sem["s_dcol"], 1)
                for g in range(4 if TRUNC >= 3 else 0):
                    if g == 1:
                        v.wait_ge(sem["s_b4"], 16)
                        mzp(v.reduce_sum(sc11.ap()[:, 11:12], row128.ap(),
                                         AX.X))
                        mzf()
                        v.tensor_scalar(sc11.ap()[:, 4:5], sc11.ap()[:, 11:12],
                                        float(0.7 / (D * D)), None, Alu.mult)
                        mzp(v.tensor_scalar(sc11.ap()[:, 5:6],
                                            sc11.ap()[:, 11:12],
                                            float(-0.7 / (D * D)), None,
                                            Alu.mult))
                        mzf()
                        v.sem_inc(sem["s_b5"], 1)
                    for rt in range(4 * g, 4 * g + 4):
                        v.wait_ge(sem["s_xn2"], rt + 1)
                        v.reduce_max(lmax.ap()[:, rt:rt + 1],
                                     xn2r.ap()[:, rt % 2, :], AX.X,
                                     apply_absolute_value=True).then_inc(sem["s_lm"], 1)
                        st = rt - 2
                        if st >= 0 and TRUNC >= 4 and T4GP:
                            v.wait_ge(sem["s_p2"], st + 1)
                            v.tensor_tensor(xbuf.ap()[:, st, :],
                                            xbuf.ap()[:, st, :],
                                            gmr.ap()[:, st % 2, :],
                                            Alu.add).then_inc(sem["s_s1"], 1)
                        if rt >= 13 and TRUNC >= 6:
                            c = rt - 13
                            v.wait_ge(sem["s_sg"], c + 1)
                            v.tensor_tensor(mTb.ap()[:, c, :],
                                            sgn2.ap()[:, 0, :],
                                            sgn2.ap()[:, 1, :],
                                            Alu.add).then_inc(sem["s_m"], 1)
                    if TRUNC < 4 or not T4AB:
                        continue
                    sl2 = slice(g * 512, (g + 1) * 512)
                    v.wait_ge(sem["s_abe"], g + 1)
                    if g >= 1:
                        v.wait_ge(sem["s_hsb"], g)
                    ih = 0.0 if g == 0 else hscr.ap()[:, 511:512]
                    ip = 1.0 if g == 0 else pscr.ap()[:, 511:512]
                    v.tensor_tensor_scan(hscr.ap(), a_bf.ap()[:, sl2],
                                         b_bf.ap()[:, sl2], ih, Alu.mult,
                                         Alu.add)
                    v.tensor_tensor_scan(pscr.ap(), a_bf.ap()[:, sl2],
                                         a_bf.ap()[:, sl2], ip, Alu.mult,
                                         Alu.bypass).then_inc(sem["s_scan"], 1)
                for st in ((NT - 2, NT - 1) if TRUNC >= 4 and T4GP else ()):
                    v.wait_ge(sem["s_p2"], st + 1)
                    v.tensor_tensor(xbuf.ap()[:, st, :], xbuf.ap()[:, st, :],
                                    gmr.ap()[:, st % 2, :],
                                    Alu.add).then_inc(sem["s_s1"], 1)
                if TRUNC >= 6:
                    v.wait_ge(sem["s_lm"], 16)
                    v.reduce_max(lmax1.ap(), lmax.ap(), AX.X).then_inc(sem["s_b1"], 1)
                    v.wait_ge(sem["s_b2"], 16)
                    v.reduce_max(sc11.ap()[:, 0:1], row128.ap(), AX.X).then_inc(sem["s_b3"], 1)
                    for c in range(3, KC):
                        v.wait_ge(sem["s_sg"], c + 1)
                        v.tensor_tensor(mTb.ap()[:, c, :], sgn2.ap()[:, 0, :],
                                        sgn2.ap()[:, 1, :],
                                        Alu.add).then_inc(sem["s_m"], 1)
                if TRUNC >= 5:
                    v.wait_ge(sem["s_sum"], 1)
                    v.tensor_scalar(wchb.ap(), wcTb.ap(), hinc.ap(), None,
                                    Alu.mult).then_inc(sem["s_wch"], 1)
                def sinv_chain():
                    v.wait_ge(sem["s_gm2"], 1)
                    mzp(v.reciprocal(sc11.ap()[:, 1:2], sc11.ap()[:, 0:1]))
                    mzf()
                    v.tensor_scalar(sc11.ap()[:, 2:3], sc11.ap()[:, 1:2], 127.0,
                                    None, Alu.mult)
                    mzp(v.tensor_scalar(sc11.ap()[:, 3:4], sc11.ap()[:, 0:1],
                                        float(np.float32(1.0)
                                              / np.float32(127.0)),
                                        None, Alu.mult))
                    mzf()
                    v.sem_inc(sem["s_sv1"], 1)

                def alpha_part1():
                    v.wait_ge(sem["s_sg"], KC)
                    mzp(v.tensor_tensor(nmc.ap(), nmc.ap(), sqc.ap(), Alu.add))
                    mzf()
                    v.reduce_sum(col1.ap(), nmc.ap(), AX.X).then_inc(sem["s_acol"], 1)

                def alpha_part2():
                    v.wait_ge(sem["s_b6"], 16)
                    v.reduce_sum(sc11.ap()[:, 4:5], row128.ap(), AX.X)
                    v.wait_ge(sem["s_sq2"], KC)
                    v.reduce_sum(col1.ap(), awc.ap(), AX.X).then_inc(sem["s_acol"], 2)

                def alpha_part3():
                    v.wait_ge(sem["s_b6"], 32)
                    mzp(v.reduce_sum(sc11.ap()[:, 5:6], row128.ap(), AX.X))
                    mzf()
                    mzp(v.tensor_tensor(sc11.ap()[:, 6:7], sc11.ap()[:, 5:6],
                                        deltaP.ap()[0:1, 0:1], Alu.mult))
                    mzf()
                    mzp(v.tensor_tensor(sc11.ap()[:, 6:7], sc11.ap()[:, 6:7],
                                        sc11.ap()[:, 4:5], Alu.add))
                    mzf()
                    mzp(v.tensor_scalar_max(sc11.ap()[:, 7:8],
                                            sc11.ap()[:, 5:6], 1.0))
                    mzf()
                    mzp(v.reciprocal(sc11.ap()[:, 7:8], sc11.ap()[:, 7:8]))
                    mzf()
                    mzp(v.tensor_tensor(sc11.ap()[:, 8:9], sc11.ap()[:, 6:7],
                                        sc11.ap()[:, 7:8], Alu.mult))
                    mzf()
                    mzp(v.tensor_tensor(sc11.ap()[:, 9:10], sc11.ap()[:, 8:9],
                                        sc11.ap()[:, 3:4], Alu.mult))
                    mzf()
                    mzp(v.tensor_scalar(sc11.ap()[:, 10:11], sc11.ap()[:, 9:10],
                                        0.5, None, Alu.mult))
                    mzf()
                    v.sem_inc(sem["s_alpha"], 1)

                def round_q(rt):
                    if rt < 2:
                        v.wait_ge(sem[f"s_xTp{7}"], 32)
                        v.wait_ge(sem["s_sq2"], KC)
                    v.wait_ge(sem[f"s_rlk{rt // 2}"], 16)
                    v.wait_ge(sem["s_sinv"], 16)
                    if rt >= 2:
                        v.wait_ge(sem[f"s_qTp{(rt - 2) // 2}"], 32)
                    v.tensor_scalar(xn2r.ap()[:, rt % 2, :],
                                    xn2r.ap()[:, rt % 2, :], sinvc.ap(),
                                    magicc.ap(), Alu.mult, Alu.add)
                    v.tensor_scalar(xnbq.ap()[:, rt % 2, :],
                                    xn2r.ap()[:, rt % 2, :], -MAGIC, None,
                                    Alu.add).then_inc(sem["s_q"], 1)

                for tt in range(NT if TRUNC >= 5 else 0):
                    for h in (0, 1):
                        p = 2 * tt + h
                        ubuf = uP.ap() if p % 2 == 0 else dxP.ap()[:, 0:512]
                        v.wait_ge(sem["s_u"], p + 1)
                        v.wait_ge(sem["s_g"], tt + 1)
                        v.scalar_tensor_tensor(
                            gview(tt)[:, h * 512:(h + 1) * 512], ubuf, 1.0,
                            gview(tt)[:, h * 512:(h + 1) * 512], Alu.mult,
                            Alu.mult).then_inc(sem["s_p1"], 1)
                    v.wait_ge(sem["s_s1"], tt + 1)
                    if DEBUG_DUMPS:
                        v.wait_ge(sem["s_dbgA"], 16 * NT)
                    v.tensor_tensor(xbuf.ap()[:, tt, :], xbuf.ap()[:, tt, :],
                                    gview(tt), Alu.add).then_inc(sem["s_s2"], 1)
                    if TRUNC < 6:
                        continue
                    if tt == 2:
                        alpha_part1()
                    if tt == 3:
                        sinv_chain()
                    if tt == 4:
                        alpha_part2()
                    if tt == 6:
                        alpha_part3()
                    rt = tt - 5
                    if rt >= 0:
                        round_q(rt)
                if TRUNC >= 6:
                    for rt in range(NT - 5, NT):
                        round_q(rt)
                for tt in range(NT if TRUNC >= 6 else 0):
                    v.wait_ge(sem["s_ffn"], tt + 1)
                    v.wait_ge(sem["s_cb"], 16)
                    if DEBUG_DUMPS:
                        v.wait_ge(sem["s_dbgB"], 16 * NT)
                    par = [zgA, zgB][tt % 2]
                    v.scalar_tensor_tensor(xbuf.ap()[:, tt, :], par.ap(),
                                           cbc.ap(), xbuf.ap()[:, tt, :],
                                           Alu.mult, Alu.add).then_inc(sem["s_out"], 1)

            @blk.gpsimd
            def _(gp):
                nio = [0]

                def gdma(out, in_):
                    gp.dma_start(out=out, in_=in_).then_inc(sem["s_io"], 16)
                    nio[0] += 16
                    gp.wait_ge(sem["s_io"], nio[0])

                for kc in range(KC):
                    gp.dma_start(out=wabTb.ap()[:, kc, :],
                                 in_=wabT_d[kc * 128:(kc + 1) * 128, :]).then_inc(sem["s_w"], 16)
                gp.dma_start(out=wcTb.ap(), in_=wcT_d).then_inc(sem["s_w"], 16)
                gp.dma_start(out=bDb.ap(), in_=bD_d).then_inc(sem["s_w"], 16)
                gp.dma_start(out=bgb.ap(), in_=bg_d).then_inc(sem["s_w"], 16)
                gp.dma_start(out=myoff_s.ap(), in_=myoff_d).then_inc(sem["s_w"], 16)
                gp.dma_start(out=partoff_s.ap(), in_=partoff_d).then_inc(sem["s_w"], 16)
                for kc in range(KC):
                    gp.dma_start(out=wgTb.ap()[:, kc, :],
                                 in_=wgT_d[kc * 128:(kc + 1) * 128, :]).then_inc(sem["s_w"], 16)
                    gp.dma_start(out=wdTb.ap()[:, kc, :],
                                 in_=wdT_d[kc * 128:(kc + 1) * 128, :]).then_inc(sem["s_w"], 16)
                for k in range(KC if TRUNC >= 1 else 0):
                    if k >= 1:
                        gp.wait_ge(sem["s_aw"], k)
                    gp.dma_start(out=wfr.ap()[:, 0, :],
                                 in_=wfT_d[k * 128:(k + 1) * 128, :]).then_inc(sem["s_wf1"], 16)
                moff = None
                if TRUNC >= 5:
                    gp.wait_ge(sem["s_mz"], 1)
                    gp.wait_ge(sem["s_w"], 16 * 29)
                    gdma(cc1_in.ap(), z840.ap())
                    moff = gp.value_load(myoff_s.ap())
                for tt in range(NT if TRUNC >= 3 else 0):
                    pt = tt - 2
                    if 0 <= pt and TRUNC >= 4 and T4GP:
                        gp.wait_ge(sem["s_gm"], pt + 1)
                        gp.wait_ge(sem["s_dxe"], pt + 1)
                        gp.tensor_tensor(gmr.ap()[:, pt % 2, :],
                                         gmr.ap()[:, pt % 2, :],
                                         dxe.ap()[:, pt % 2, :],
                                         Alu.mult).then_inc(sem["s_p2"], 1)
                    gp.wait_ge(sem["s_z"], tt + 1)
                    gp.wait_ge(sem["s_gb"], 32)
                    gp.tensor_tensor(xn2r.ap()[:, tt % 2, :],
                                     xn2r.ap()[:, tt % 2, :], gammaB.ap(),
                                     Alu.mult)
                    gp.tensor_tensor(xn2r.ap()[:, tt % 2, :],
                                     xn2r.ap()[:, tt % 2, :], betaB.ap(),
                                     Alu.add).then_inc(sem["s_xn2"], 1)
                    if tt == 2:
                        gp.wait_ge(sem["s_dcol"], 1)
                        gdma(scrA.ap().rearrange("(p one) -> p one", one=1),
                             col1.ap())
                        gp.dma_start(out=row128.ap(),
                                     in_=scrA.ap().rearrange(
                                         "(one p) -> one p", one=1)).then_inc(sem["s_b4"], 16)
                    if tt == 4:
                        gp.wait_ge(sem["s_b5"], 1)
                        gp.dma_start(out=scrB.ap()[4:6].rearrange("(a b) -> a b", a=1),
                                     in_=sc11.ap()[:, 4:6]).then_inc(sem["s_dscr"], 16)
                    if 12 <= tt and TRUNC >= 6:
                        k = tt - 12
                        gp.wait_ge(sem["s_aw"], KC)
                        if k >= 2:
                            gp.wait_ge(sem["s_sg"], k - 1)
                        gp.dma_start(out=wfr.ap()[:, k % 2, :],
                                     in_=wfT_d[k * 128:(k + 1) * 128, :]).then_inc(
                                         sem["s_wf2" if k % 2 == 0 else "s_wf1"], 16)
                if TRUNC >= 5:
                    gp.wait_ge(sem["s_scan"], 4)
                    gdma(scrH.ap().rearrange("(p one) -> p one", one=1),
                         hscr.ap()[:, 511:512])
                    gdma(cc1_in.ap().rearrange("r c -> (r c)")[bass.ds(moff, 16)],
                         scrH.ap())
                    gp.collective_compute(
                        "AllReduce", Alu.add,
                        replica_groups=[list(range(NCORE))],
                        ins=[cc1_in.ap().opt()],
                        outs=[cc1_out.ap().opt()]).then_inc(sem["s_cc1"], 1)
                for pt in ((NT - 2, NT - 1) if TRUNC >= 4 and T4GP else ()):
                    gp.wait_ge(sem["s_gm"], pt + 1)
                    gp.wait_ge(sem["s_dxe"], pt + 1)
                    gp.tensor_tensor(gmr.ap()[:, pt % 2, :],
                                     gmr.ap()[:, pt % 2, :],
                                     dxe.ap()[:, pt % 2, :],
                                     Alu.mult).then_inc(sem["s_p2"], 1)
                if TRUNC >= 6:
                    gp.wait_ge(sem["s_b1"], 1)
                    gp.wait_ge(sem["s_b5"], 1)
                    gdma(scrA.ap().rearrange("(p one) -> p one", one=1), lmax1.ap())
                    gp.dma_start(out=row128.ap(),
                                 in_=scrA.ap().rearrange("(one p) -> one p",
                                                         one=1)).then_inc(sem["s_b2"], 16)
                    gp.wait_ge(sem["s_b3"], 1)
                    gp.dma_start(out=scrB.ap()[0:1].rearrange("(a b) -> a b", a=1),
                                 in_=sc11.ap()[:, 0:1]).then_inc(sem["s_c2s"], 16)
                    gp.wait_ge(sem["s_c2b"], 16)
                    gp.collective_compute(
                        "AllReduce", Alu.max,
                        replica_groups=[list(range(NCORE))],
                        ins=[cc2_in.ap().opt()],
                        outs=[cc2_out.ap().opt()]).then_inc(sem["s_cc2"], 1)
                if TRUNC >= 6:
                    for k in range(4, KC):
                        gp.wait_ge(sem["s_sg"], k - 1)
                        gp.dma_start(out=wfr.ap()[:, k % 2, :],
                                     in_=wfT_d[k * 128:(k + 1) * 128, :]).then_inc(
                                         sem["s_wf2" if k % 2 == 0 else "s_wf1"], 16)
                if TRUNC >= 6:
                    for j in range(KC):
                        gp.wait_ge(sem[f"s_spk{j}"], 16)
                    gp.dma_start(
                        out=xn2r.ap()[:, 0:2, :],
                        in_=xn2_d[0:256, :].rearrange(
                            "(s p) d -> p s d", s=2)).then_inc(sem["s_rlk0"], 16)
                if TRUNC >= 5:
                    gp.wait_ge(sem["s_cc1"], 1)
                    poff = gp.value_load(partoff_s.ap())
                    gdma(scrH2.ap(),
                         cc1_out.ap().rearrange("r c -> (r c)")[bass.ds(poff, 16)])
                    gdma(hinc.ap(), scrH2.ap().rearrange("(p one) -> p one", one=1))
                    gp.sem_inc(sem["s_sum"], 1)
                if TRUNC >= 6:
                    gp.wait_ge(sem["s_cc2"], 1)
                    gdma(gath2.ap(), cc2_out.ap())
                    gp.tensor_scalar_max(sc11.ap()[:, 0:1], gath2.ap()[0:1, 0:1],
                                         1e-8).then_inc(sem["s_gm2"], 1)
                    gp.wait_ge(sem["s_sv1"], 1)
                    gp.dma_start(out=scrB.ap()[1:2].rearrange("(a b) -> a b", a=1),
                                 in_=sc11.ap()[:, 2:3]).then_inc(sem["s_svs"], 16)
                    gp.wait_ge(sem["s_acol"], 1)
                    gdma(scrA.ap().rearrange("(p one) -> p one", one=1), col1.ap())
                    gp.dma_start(out=row128.ap(),
                                 in_=scrA.ap().rearrange("(one p) -> one p",
                                                         one=1)).then_inc(sem["s_b6"], 16)
                    gp.wait_ge(sem["s_acol"], 3)
                    gdma(scrA.ap().rearrange("(p one) -> p one", one=1), col1.ap())
                    gp.dma_start(out=row128.ap(),
                                 in_=scrA.ap().rearrange("(one p) -> one p",
                                                         one=1)).then_inc(sem["s_b6"], 16)
                    gp.wait_ge(sem["s_alpha"], 1)
                    gp.dma_start(out=scrB.ap()[2:3].rearrange("(a b) -> a b", a=1),
                                 in_=sc11.ap()[:, 10:11]).then_inc(sem["s_cbs"], 16)
                    for k in range(1, KC):
                        gp.wait_ge(sem["s_q"], 2 * k)
                        gp.dma_start(
                            out=xn2r.ap()[:, 0:2, :],
                            in_=xn2_d[k * 256:(k + 1) * 256, :].rearrange(
                                "(s p) d -> p s d", s=2)).then_inc(sem[f"s_rlk{k}"], 16)

            @blk.tensor
            def _(t):
                if TRUNC >= 4:
                    t.wait_ge(sem["s_w"], 16 * 29)
                    t.wait_ge(sem["s_mz"], 1)
                for tt in range(NT if TRUNC >= 4 else 0):
                    g = tt // 4
                    par = [zgA, zgB][tt % 2]
                    t.wait_ge(sem[f"s_xTp{tt // 2}"], 32)
                    if tt >= 2 and T4G and T4G_G:
                        t.wait_ge(sem["s_g"], tt - 1)
                    if tt >= 2 and T4G and T4G_M:
                        t.wait_ge(sem["s_gm"], tt - 1)
                    if tt >= 1 and T4G and T4G_D:
                        t.wait_ge(sem["s_dxe"], tt)
                    for kc in range(KC):
                        lhs = xnTq.ap()[:, tt, kc, :]
                        for f in (0, 1):
                            fsl = slice(f * 512, (f + 1) * 512)
                            t.matmul(par.ap()[:, fsl], lhs,
                                     wgTb.ap()[:, kc, fsl],
                                     start=(kc == 0), stop=False)
                        for f in (0, 1):
                            fsl = slice(f * 512, (f + 1) * 512)
                            t.matmul(dxP.ap()[:, fsl], lhs,
                                     wdTb.ap()[:, kc, fsl],
                                     start=(kc == 0), stop=False)
                    for f in (0, 1):
                        fsl = slice(f * 512, (f + 1) * 512)
                        mm = t.matmul(par.ap()[:, fsl], onesb.ap(),
                                      bgb.ap()[:, fsl], start=False, stop=True)
                        if f == 1:
                            mm.then_inc(sem["s_zg"], 1)
                    for f in (0, 1):
                        fsl = slice(f * 512, (f + 1) * 512)
                        mm = t.matmul(dxP.ap()[:, fsl], onesb.ap(),
                                      bDb.ap()[:, fsl], start=False, stop=True)
                        if f == 1:
                            mm.then_inc(sem["s_dx"], 1)
                    if tt % 4 == 1:
                        t.wait_ge(sem[f"s_xTp{2 * g + 1}"], 32)
                        if g >= 1 and T4AB:
                            t.wait_ge(sem["s_abe"], g)
                        for kc in range(KC):
                            mm = t.matmul(abP.ap(), wabTb.ap()[:, kc, :],
                                          xnTq.ap()[:, 4 * g:4 * g + 4, kc, :],
                                          start=(kc == 0), stop=(kc == KC - 1))
                            if kc == KC - 1:
                                mm.then_inc(sem["s_ab"], 1)
                if TRUNC >= 5:
                    t.wait_ge(sem["s_wch"], 1)
                    t.wait_ge(sem["s_hsb"], 4)
                    t.wait_ge(sem["s_dxe"], NT)
                for ut in range(NT if TRUNC >= 5 else 0):
                    for h in (0, 1):
                        fsl = slice(h * 512, (h + 1) * 512)
                        p = 2 * ut + h
                        ubuf = uP.ap() if p % 2 == 0 else dxP.ap()[:, 0:512]
                        if p >= 2:
                            t.wait_ge(sem["s_p1"], p - 1)
                        t.matmul(ubuf,
                                 hsb.ap()[:, ut * 128:(ut + 1) * 128],
                                 wcTb.ap()[:, fsl], start=True, stop=False)
                        t.matmul(ubuf,
                                 Pb.ap()[:, ut * 128:(ut + 1) * 128],
                                 wchb.ap()[:, fsl], start=False,
                                 stop=True).then_inc(sem["s_u"], 1)
                if TRUNC >= 6:
                    t.wait_ge(sem["s_m"], KC)
                for tt in range(NT if TRUNC >= 6 else 0):
                    par = [zgA, zgB][tt % 2]
                    t.wait_ge(sem[f"s_qTp{tt // 2}"], 32)
                    if tt < 2:
                        t.wait_ge(sem["s_g"], NT)
                        t.wait_ge(sem["s_gm"], NT)
                    else:
                        t.wait_ge(sem["s_out"], tt - 1)
                    for kc in range(KC):
                        lhs = xnTq.ap()[:, tt, kc, :]
                        for f in (0, 1):
                            fsl = slice(f * 512, (f + 1) * 512)
                            mm = t.matmul(par.ap()[:, fsl], lhs,
                                          mTb.ap()[:, kc, fsl],
                                          start=(kc == 0), stop=(kc == KC - 1))
                            if kc == KC - 1 and f == 1:
                                mm.then_inc(sem["s_ffn"], 1)

    return nc



_CACHED = {}


def kernel(L0_attn_x, L0_attn_WA, L0_attn_WB, L0_attn_WC, L0_attn_WD,
           L0_attn_bD, L0_attn_Wg, L0_attn_bg, L0_ffn_weight,
           L0_ffn_gamma, L0_ffn_beta):
    x = np.ascontiguousarray(np.asarray(L0_attn_x, np.float32))
    B, Sq, D_ = x.shape
    wgT = np.ascontiguousarray(np.asarray(L0_attn_Wg, np.float32).T)
    wdT = np.ascontiguousarray(np.asarray(L0_attn_WD, np.float32).T)
    wfT = np.ascontiguousarray(np.asarray(L0_ffn_weight, np.float32).T)
    wabT = np.ascontiguousarray(
        np.concatenate([np.asarray(L0_attn_WA, np.float32),
                        np.zeros((16, D), np.float32),
                        np.asarray(L0_attn_WB, np.float32)], axis=0).T)
    wcT = np.ascontiguousarray(np.asarray(L0_attn_WC, np.float32).T)
    gamma = np.ascontiguousarray(np.asarray(L0_ffn_gamma, np.float32).reshape(1, D))
    beta = np.ascontiguousarray(np.asarray(L0_ffn_beta, np.float32).reshape(1, D))
    bD = np.ascontiguousarray(np.asarray(L0_attn_bD, np.float32).reshape(1, D))
    bg = np.ascontiguousarray(np.asarray(L0_attn_bg, np.float32).reshape(1, D))

    if "nc" not in _CACHED:
        _CACHED["nc"] = build_nc()
    nc = _CACHED["nc"]

    in_maps = []
    for c in range(NCORE):
        b, h = c // 2, c % 2
        in_maps.append(dict(
            x=np.ascontiguousarray(x[b, h * TC:(h + 1) * TC, :]),
            wgT=wgT, wdT=wdT, wfT=wfT, wabT=wabT, wcT=wcT,
            gamma=gamma, beta=beta, bD=bD, bg=bg,
            myoff=np.array([[c * 40 + 2]], dtype=np.int32),
            partoff=np.array([[(c - 1) * 40 + 2 if h == 1 else 20]],
                             dtype=np.int32),
        ))
    res = run_bass_kernel_spmd(nc, in_maps, core_ids=list(range(NCORE)))
    out = np.empty((B, Sq, D), np.float32)
    for c in range(NCORE):
        b, h = c // 2, c % 2
        out[b, h * TC:(h + 1) * TC, :] = res.results[c]["out"]
    return out


# version 15
# speedup vs baseline: 1.0128x; 1.0128x over previous
import contextlib
import numpy as np
import concourse.bass as bass
import concourse.mybir as mybir
from concourse.bass_utils import run_bass_kernel_spmd

F32 = mybir.dt.float32
BF16 = mybir.dt.bfloat16
I32 = mybir.dt.int32
Alu = mybir.AluOpType
Act = mybir.ActivationFunctionType
AX = mybir.AxisListType

NCORE = 8
DEBUG_DUMPS = False
TRUNC = 99
T4TE = True
T4AB = True
T4G = True
T4GP = True
T4GV = True
T4G_G = True
T4G_M = True
T4G_D = True
RACECHECK = False
D = 1024
N = 16
TC = 2048
NT = 16
KC = 8
MAGIC = 12582912.0
RD = float(np.float32(1.0 / 1024.0))


def build_nc():
    nc = bass.Bass("TRN2", target_bir_lowering=False,
                   detect_race_conditions=RACECHECK)

    def din(name, shape, dt=F32):
        return nc.dram_tensor(name, shape, dt, kind="ExternalInput").ap()

    x_d = din("x", [TC, D])
    wgT_d = din("wgT", [D, D])
    wdT_d = din("wdT", [D, D])
    wfT_d = din("wfT", [D, D])
    wabT_d = din("wabT", [D, 3 * N])
    wcT_d = din("wcT", [N, D])
    gamma_d = din("gamma", [1, D])
    beta_d = din("beta", [1, D])
    bD_d = din("bD", [1, D])
    bg_d = din("bg", [1, D])
    myoff_d = din("myoff", [1, 1], I32)
    partoff_d = din("partoff", [1, 1], I32)
    out_d = nc.dram_tensor("out", [TC, D], F32, kind="ExternalOutput").ap()
    if DEBUG_DUMPS:
        dbgA_d = nc.dram_tensor("dbgA", [TC, D], F32, kind="ExternalOutput").ap()
        dbgB_d = nc.dram_tensor("dbgB", [TC, D], F32, kind="ExternalOutput").ap()
        dbgS_d = nc.dram_tensor("dbgS", [1, 16], F32, kind="ExternalOutput").ap()
        dbgM_d = nc.dram_tensor("dbgM", [128, KC * D], BF16,
                                kind="ExternalOutput").ap()
        dbgQ_d = nc.dram_tensor("dbgQ", [128, NT * KC * 128], BF16,
                                kind="ExternalOutput").ap()

    xn2_d = nc.dram_tensor("xn2d", [TC, D], F32).ap()
    cc1_in = nc.dram_tensor("cc1_in", [8, 40], F32)
    cc1_out = nc.dram_tensor("cc1_out", [8, 40], F32)
    cc2_in = nc.dram_tensor("cc2_in", [8, 4], F32)
    cc2_out = nc.dram_tensor("cc2_out", [8, 4], F32)
    scrA = nc.dram_tensor("scrA", [128], F32)
    scrH = nc.dram_tensor("scrH", [16], F32)
    scrH2 = nc.dram_tensor("scrH2", [16], F32)
    scrB = nc.dram_tensor("scrB", [8], F32)

    S = lambda *a, **k: nc.alloc_sbuf_tensor(*a, **k)
    xbuf = S("xbuf", [128, NT, D], F32)
    xnTq = S("xnTq", [128, NT, KC, 128], BF16, is_transpose_dest=True)
    wgTb = S("wgTb", [128, KC, D], BF16)
    wdTb = S("wdTb", [128, KC, D], BF16)
    mTb = S("mTb", [128, KC, D], BF16)
    wcTb = S("wcTb", [N, D], BF16)
    wchb = S("wchb", [N, D], BF16)
    wabTb = S("wabTb", [128, KC, 3 * N], BF16)
    onesb = S("onesb", [1, 128], BF16)
    bDb = S("bDb", [1, D], BF16)
    bgb = S("bgb", [1, D], BF16)
    gammaB = S("gammaB", [128, D], F32)
    betaB = S("betaB", [128, D], F32)
    xn2r = S("xn2r", [128, 2, D], F32)
    xnbq = S("xnbq", [128, 2, D], BF16)
    gmr = S("gmr", [128, 2, D], BF16)
    dxe = S("dxe", [128, 2, D], BF16)
    wfr = S("wfr", [128, 2, D], F32)
    sgn2 = S("sgn2", [128, 2, D], BF16)
    a_bf = S("a_bf", [N, TC], BF16)
    b_bf = S("b_bf", [N, TC], BF16)
    hsb = a_bf
    Pb = b_bf
    hscr = S("hscr", [N, 512], F32)
    pscr = S("pscr", [N, 512], F32)
    sumc = S("sumc", [128, NT], F32)
    ssc = S("ssc", [128, NT], F32)
    muc = S("muc", [128, NT], F32)
    invc = S("invc", [128, NT], F32)
    c2c = S("c2c", [128, NT], F32)
    nm1c = S("nm1c", [128, NT], F32)
    nm2c = S("nm2c", [128, NT], F32)
    tmpc = S("tmpc", [128, NT], F32)
    tmc2 = S("tmc2", [128, NT], F32)
    lmax = S("lmax", [128, NT], F32)
    lmax1 = S("lmax1", [128, 1], F32)
    awc = S("awc", [128, KC], F32)
    nmc = S("nmc", [128, KC], F32)
    sqc = S("sqc", [128, KC], F32)
    class _ColView:
        def __init__(self, t, j):
            self._t, self._j = t, j

        def ap(self):
            return self._t.ap()[:, self._j:self._j + 1]

    colp = S("colp", [128, 8], F32)
    col1 = _ColView(colp, 0)
    deltaP = _ColView(colp, 1)
    deltaN = _ColView(colp, 2)
    sinvc = _ColView(colp, 3)
    cbc = _ColView(colp, 4)
    magicc = _ColView(colp, 5)
    epsc = _ColView(colp, 6)
    row128 = S("row128", [1, 128], F32)
    z840 = S("z840", [8, 40], F32)
    gath2 = S("gath2", [8, 4], F32)
    hinc = S("hinc", [N, 1], F32)
    sc11 = S("sc11", [1, 12], F32)
    myoff_s = S("myoff_s", [1, 1], I32)
    partoff_s = S("partoff_s", [1, 1], I32)

    zgA = nc.alloc_psum_tensor("zgA", [128, D], F32)
    zgB = nc.alloc_psum_tensor("zgB", [128, D], F32)
    dxP = nc.alloc_psum_tensor("dxP", [128, D], F32)
    uP = nc.alloc_psum_tensor("uP", [128, 512], F32)
    abP = nc.alloc_psum_tensor("abP", [3 * N, 512], F32)


    def gview(tt):
        return xnTq.ap()[:, tt, :, :].rearrange("p k f -> p (k f)")

    ctx = contextlib.ExitStack()
    sem = {}
    for name in (["s_gb", "s_w", "s_ssq", "s_std", "s_sts", "s_std2",
                  "s_sts2", "s_stats", "s_xnb", "s_z",
                  "s_xn2", "s_lm", "s_ab", "s_abe", "s_scan",
                  "s_hsb", "s_zg", "s_dx", "s_u", "s_g", "s_gm", "s_p2",
                  "s_s1", "s_p1", "s_s2", "s_cc1", "s_wch", "s_cc2", "s_q",
                  "s_ffn", "s_out", "s_st", "s_mz",
                  "s_aw", "s_dcol", "s_b1", "s_b2", "s_b3", "s_b4",
                  "s_b5", "s_b6", "s_delta", "s_sg", "s_m",
                  "s_num", "s_sq2", "s_acol", "s_alpha", "s_gm2", "s_sv1",
                  "s_sinv", "s_cb", "s_io", "s_dscr", "s_c2s", "s_c2b",
                  "s_svs", "s_cbs", "s_io2", "s_sum", "s_dxe", "s_dbgA",
                  "s_dbgB"]
                 + ["s_wf1", "s_wf2"]
                 + [f"s_xg{i}" for i in range(4)]
                 + [f"s_xTp{i}" for i in range(NT // 2)]
                 + [f"s_qTp{i}" for i in range(NT // 2)]
                 + [f"s_spk{i}" for i in range(KC)]
                 + [f"s_rlk{i}" for i in range(KC)]):
        sem[name] = ctx.enter_context(nc.semaphore(name))

    with ctx:
        with nc.Block() as blk:

            @blk.sync
            def _(sync):
                if TRUNC >= 2:
                    for tt in range(NT):
                        sync.wait_ge(sem["s_xnb"], tt + 1)
                        sync.dma_start(
                            out=xnTq.ap()[:, tt, :, :],
                            in_=xnbq.ap()[:, tt % 2, :].rearrange(
                                "p (k f) -> p k f", k=KC),
                            transpose=True).then_inc(sem[f"s_xTp{tt // 2}"], 16)
                if TRUNC >= 3:
                    for k in range(KC):
                        if k == 4:
                            sync.wait_ge(sem["s_dscr"], 16)
                            sync.dma_start(out=deltaP.ap(),
                                           in_=scrB.ap()[4:5].rearrange(
                                               "(a b) -> a b", a=1)
                                           .broadcast_to([128, 1])).then_inc(sem["s_delta"], 16)
                            sync.dma_start(out=deltaN.ap(),
                                           in_=scrB.ap()[5:6].rearrange(
                                               "(a b) -> a b", a=1)
                                           .broadcast_to([128, 1])).then_inc(sem["s_delta"], 16)
                        sync.wait_ge(sem["s_xn2"], 2 * k + 2)
                        sl2 = slice(0, 2)
                        sync.dma_start(
                            out=xn2_d[k * 256:(k + 1) * 256, :].rearrange(
                                "(s p) d -> p s d", s=2),
                            in_=xn2r.ap()[:, sl2, :]).then_inc(sem[f"s_spk{k}"], 16)
                if TRUNC >= 6:
                    sync.wait_ge(sem["s_c2s"], 16)
                    sync.dma_start(out=cc2_in.ap(),
                                   in_=z840.ap()[0:8, 0:4]).then_inc(sem["s_io2"], 16)
                    sync.wait_ge(sem["s_io2"], 16)
                    sync.dma_start(out=cc2_in.ap()[:, 0:1],
                                   in_=scrB.ap()[0:1].rearrange("(a b) -> a b", a=1)
                                   .broadcast_to([8, 1]),
                                   allow_slow_non_contiguous=True).then_inc(sem["s_c2b"], 16)
                if DEBUG_DUMPS:
                    for tt in range(NT):
                        sync.wait_ge(sem["s_s1"], NT)
                        sync.dma_start(out=dbgA_d[tt * 128:(tt + 1) * 128, :],
                                       in_=xbuf.ap()[:, tt, :]).then_inc(sem["s_dbgA"], 16)
                if TRUNC >= 6:
                    sync.wait_ge(sem["s_svs"], 16)
                    sync.dma_start(out=sinvc.ap(),
                                   in_=scrB.ap()[1:2].rearrange(
                                       "(a b) -> a b", a=1)
                                   .broadcast_to([128, 1])).then_inc(sem["s_sinv"], 16)
                for tt in range(NT if TRUNC >= 6 else 0):
                    sync.wait_ge(sem["s_q"], tt + 1)
                    sync.wait_ge(sem["s_p1"], 2 * tt + 2)
                    sync.wait_ge(sem["s_s2"], tt + 1)
                    sync.dma_start(
                        out=xnTq.ap()[:, tt, :, :],
                        in_=xnbq.ap()[:, tt % 2, :].rearrange(
                            "p (k f) -> p k f", k=KC),
                        transpose=True).then_inc(sem[f"s_qTp{tt // 2}"], 16)
                if DEBUG_DUMPS:
                    for tt in range(NT):
                        sync.wait_ge(sem["s_s2"], NT)
                        sync.dma_start(out=dbgB_d[tt * 128:(tt + 1) * 128, :],
                                       in_=xbuf.ap()[:, tt, :]).then_inc(sem["s_dbgB"], 16)
                if TRUNC >= 6:
                    sync.wait_ge(sem["s_cbs"], 16)
                    sync.dma_start(out=cbc.ap(),
                                   in_=scrB.ap()[2:3].rearrange("(a b) -> a b", a=1)
                                   .broadcast_to([128, 1])).then_inc(sem["s_cb"], 16)
                if DEBUG_DUMPS:
                    sync.wait_ge(sem["s_cb"], 16)
                    sync.wait_ge(sem["s_sinv"], 16)
                    sync.wait_ge(sem["s_delta"], 32)
                    sync.dma_start(out=dbgS_d[0:1, 0:12],
                                   in_=sc11.ap()).then_inc(sem["s_io2"], 16)
                    sync.dma_start(out=dbgS_d[0:1, 12:13],
                                   in_=deltaP.ap()[0:1, :]).then_inc(sem["s_io2"], 16)
                    sync.dma_start(out=dbgS_d[0:1, 13:14],
                                   in_=sinvc.ap()[0:1, :]).then_inc(sem["s_io2"], 16)
                    sync.dma_start(out=dbgS_d[0:1, 14:15],
                                   in_=cbc.ap()[0:1, :]).then_inc(sem["s_io2"], 16)
                    sync.dma_start(out=dbgS_d[0:1, 15:16],
                                   in_=gath2.ap()[0:1, 0:1]).then_inc(sem["s_io2"], 16)
                    sync.wait_ge(sem["s_io2"], 96)
                for tt in range(0, NT, 2):
                    if TRUNC >= 6:
                        sync.wait_ge(sem["s_out"], tt + 1)
                    elif TRUNC == 5:
                        sync.wait_ge(sem["s_s2"], tt + 1)
                    elif TRUNC == 4 and T4GP:
                        sync.wait_ge(sem["s_s1"], tt + 1)
                    elif TRUNC == 4:
                        sync.wait_ge(sem["s_dx"], tt + 1)
                    elif TRUNC == 3:
                        sync.wait_ge(sem["s_xn2"], tt + 1)
                    elif TRUNC == 2:
                        sync.wait_ge(sem[f"s_xTp{tt // 2}"], 32)
                    elif TRUNC == 1:
                        sync.wait_ge(sem["s_xnb"], tt + 1)
                    else:
                        sync.wait_ge(sem[f"s_xg{tt // 4}"], 64)
                    sync.dma_start(out=out_d[tt * 128:(tt + 1) * 128, :],
                                   in_=xbuf.ap()[:, tt, :]).then_inc(sem["s_st"], 16)
                sync.wait_ge(sem["s_st"], 16 * NT)
                if DEBUG_DUMPS:
                    sync.dma_start(out=dbgM_d,
                                   in_=mTb.ap().rearrange(
                                       "p k d -> p (k d)")).then_inc(sem["s_io2"], 16)
                    sync.dma_start(out=dbgQ_d,
                                   in_=xnTq.ap().rearrange(
                                       "p a b c -> p (a b c)")).then_inc(sem["s_io2"], 16)
                    sync.wait_ge(sem["s_io2"], 128)

            @blk.scalar
            def _(sc):
                sc.dma_start(out=gammaB.ap(),
                             in_=gamma_d.broadcast_to([128, D])).then_inc(sem["s_gb"], 16)
                sc.dma_start(out=betaB.ap(),
                             in_=beta_d.broadcast_to([128, D])).then_inc(sem["s_gb"], 16)
                for tt in range(NT):
                    sc.dma_start(out=xbuf.ap()[:, tt, :],
                                 in_=x_d[tt * 128:(tt + 1) * 128, :]).then_inc(
                                     sem[f"s_xg{tt // 4}"], 16)
                for g in range(4 if TRUNC >= 1 else 0):
                    sc.wait_ge(sem[f"s_xg{g}"], 64)
                    for t in range(4 * g, 4 * g + 4):
                        if t >= 1:
                            sc.wait_ge(sem["s_ssq"], t)
                        if g >= 1:
                            sc.wait_ge(sem["s_aw"], 2 * g)
                        sc.activation(xn2r.ap()[:, 0, :], xbuf.ap()[:, t, :],
                                      Act.Square, bias=0.0, scale=1.0,
                                      accum_out=ssc.ap()[:, t:t + 1]).then_inc(sem["s_ssq"], 1)
                    sl = slice(4 * g, 4 * g + 4)
                    sc.wait_ge(sem["s_std"], g + 1)
                    sc.activation(invc.ap()[:, sl], tmpc.ap()[:, sl],
                                  Act.Sqrt, bias=epsc.ap(), scale=1.0).then_inc(sem["s_sts"], 1)
                    sc.wait_ge(sem["s_std2"], g + 1)
                    sc.activation(c2c.ap()[:, sl], tmc2.ap()[:, sl],
                                  Act.Sqrt, bias=epsc.ap(), scale=1.0).then_inc(sem["s_sts2"], 1)
                    for k in (2 * g, 2 * g + 1):
                        sc.wait_ge(sem["s_wf1"], 16 * (k + 1))
                        sc.wait_ge(sem["s_ssq"], 4 * g + 4)
                        sc.activation(xn2r.ap()[:, 1, :], wfr.ap()[:, 0, :],
                                      Act.Abs, bias=0.0, scale=1.0,
                                      accum_out=awc.ap()[:, k:k + 1]).then_inc(sem["s_aw"], 1)
                for tt in range(NT + 11 if TRUNC >= 3 else 0):
                    zt = tt
                    if 0 <= zt < NT:
                        sc.wait_ge(sem["s_stats"], zt // 4 + 1)
                        if zt >= 2:
                            sc.wait_ge(sem[f"s_spk{(zt - 2) // 2}"], 16)
                            sc.wait_ge(sem["s_lm"], zt - 1)
                        sc.activation(xn2r.ap()[:, zt % 2, :], xbuf.ap()[:, zt, :],
                                      Act.Identity, bias=nm2c.ap()[:, zt:zt + 1],
                                      scale=c2c.ap()[:, zt:zt + 1]).then_inc(sem["s_z"], 1)
                    et = tt - 2
                    if 0 <= et < NT and TRUNC >= 4 and T4G:
                        par = [zgA, zgB][et % 2]
                        sc.wait_ge(sem["s_dx"], et + 1)
                        if et >= 2 and T4GP:
                            sc.wait_ge(sem["s_s1"], et - 1)
                        if T4G_M:
                            sc.activation(gmr.ap()[:, et % 2, :], par.ap(),
                                          Act.Sigmoid, bias=0.0,
                                          scale=-1.0).then_inc(sem["s_gm"], 1)
                        if T4G_D:
                            sc.copy(dxe.ap()[:, et % 2, 0:512],
                                    dxP.ap()[:, 0:512])
                            sc.copy(dxe.ap()[:, et % 2, 512:1024],
                                    dxP.ap()[:, 512:1024]).then_inc(sem["s_dxe"], 1)
                        if T4G_G:
                            if et % 4 <= 1:
                                sc.wait_ge(sem["s_ab"], et // 4 + 1)
                            gdst = gview(et) if T4GV else xnbq.ap()[:, et % 2, :]
                            sc.activation(gdst, par.ap(), Act.Sigmoid,
                                          bias=0.0, scale=1.0).then_inc(sem["s_g"], 1)
                    if tt % 4 == 0 and 1 <= tt // 4 <= 4 and TRUNC >= 4 and T4AB:
                        g0 = tt // 4 - 1
                        sc.wait_ge(sem["s_ab"], g0 + 1)
                        sc.activation(a_bf.ap()[:, g0 * 512:(g0 + 1) * 512],
                                      abP.ap()[0:N, :], Act.Tanh, bias=0.0,
                                      scale=1.0)
                        sc.copy(b_bf.ap()[:, g0 * 512:(g0 + 1) * 512],
                                abP.ap()[2 * N:3 * N, :]).then_inc(sem["s_abe"], 1)
                    if tt % 4 == 2 and 1 <= (tt - 2) // 4 <= 4 and TRUNC >= 4 and T4AB:
                        g1 = (tt - 2) // 4 - 1
                        sc.wait_ge(sem["s_scan"], g1 + 1)
                        sc.copy(hsb.ap()[:, g1 * 512:(g1 + 1) * 512], hscr.ap())
                        sc.copy(Pb.ap()[:, g1 * 512:(g1 + 1) * 512],
                                pscr.ap()).then_inc(sem["s_hsb"], 1)
                    k = tt - 13
                    if 0 <= k < KC and TRUNC >= 6:
                        wfk = wfr.ap()[:, k % 2, :]
                        if k % 2 == 0:
                            sc.wait_ge(sem["s_wf2"], 16 * (k // 2 + 1))
                        else:
                            sc.wait_ge(sem["s_wf1"], 128 + 16 * (k // 2 + 1))
                        sc.wait_ge(sem["s_delta"], 32)
                        if k >= 1:
                            sc.wait_ge(sem["s_m"], k)
                        sc.activation(sgn2.ap()[:, 0, :], wfk,
                                      Act.Sign, bias=deltaN.ap(), scale=1.0)
                        sc.activation(sgn2.ap()[:, 1, :], wfk,
                                      Act.Sign, bias=deltaP.ap(), scale=1.0)
                        sc.activation(mTb.ap()[:, k, :], wfk, Act.Relu,
                                      bias=deltaN.ap(), scale=1.0,
                                      accum_out=nmc.ap()[:, k:k + 1])
                        sc.activation(mTb.ap()[:, k, :], wfk, Act.Relu,
                                      bias=deltaN.ap(), scale=-1.0,
                                      accum_out=sqc.ap()[:, k:k + 1]).then_inc(sem["s_sg"], 1)
                    k2 = tt - 15
                    if 0 <= k2 < KC and TRUNC >= 6:
                        if k2 == 0:
                            sc.wait_ge(sem[f"s_xTp{7}"], 32)
                        sc.wait_ge(sem["s_m"], k2 + 1)
                        sc.activation(xnbq.ap()[:, 0, :], mTb.ap()[:, k2, :],
                                      Act.Abs, bias=0.0, scale=0.5,
                                      accum_out=awc.ap()[:, k2:k2 + 1]).then_inc(sem["s_sq2"], 1)
                for tt in range(1, NT, 2):
                    if TRUNC >= 6:
                        sc.wait_ge(sem["s_out"], tt + 1)
                    elif TRUNC == 5:
                        sc.wait_ge(sem["s_s2"], tt + 1)
                    elif TRUNC == 4 and T4GP:
                        sc.wait_ge(sem["s_s1"], tt + 1)
                    elif TRUNC == 4:
                        sc.wait_ge(sem["s_dx"], tt + 1)
                    elif TRUNC == 3:
                        sc.wait_ge(sem["s_xn2"], tt + 1)
                    elif TRUNC == 2:
                        sc.wait_ge(sem[f"s_xTp{tt // 2}"], 32)
                    elif TRUNC == 1:
                        sc.wait_ge(sem["s_xnb"], tt + 1)
                    else:
                        sc.wait_ge(sem[f"s_xg{tt // 4}"], 64)
                    sc.dma_start(out=out_d[tt * 128:(tt + 1) * 128, :],
                                 in_=xbuf.ap()[:, tt, :]).then_inc(sem["s_st"], 16)



            @blk.vector
            def _(v):
                mzc = [1]

                def mzp(instr):
                    instr.then_inc(sem["s_mz"], 1)
                    mzc[0] += 1

                def mzf():
                    v.wait_ge(sem["s_mz"], mzc[0])

                v.memset(magicc.ap(), MAGIC)
                v.memset(epsc.ap(), 1e-5)
                v.memset(z840.ap(), 0.0)
                v.memset(onesb.ap(), 1.0).then_inc(sem["s_mz"], 1)
                for g in range(4 if TRUNC >= 1 else 0):
                    v.wait_ge(sem[f"s_xg{g}"], 64)
                    for t in range(4 * g, 4 * g + 4):
                        ins = v.reduce_sum(sumc.ap()[:, t:t + 1],
                                           xbuf.ap()[:, t, :],
                                           AX.X)
                        if t == 4 * g + 3:
                            mzp(ins)
                    sl = slice(4 * g, 4 * g + 4)
                    v.wait_ge(sem["s_ssq"], 4 * g + 4)
                    mzf()
                    mzp(v.tensor_scalar(muc.ap()[:, sl], sumc.ap()[:, sl], RD,
                                        None, Alu.mult))
                    v.tensor_scalar(tmpc.ap()[:, sl], ssc.ap()[:, sl], RD,
                                    None, Alu.mult)
                    mzf()
                    mzp(v.tensor_tensor(tmc2.ap()[:, sl], muc.ap()[:, sl],
                                        muc.ap()[:, sl], Alu.mult))
                    mzf()
                    v.tensor_tensor(tmpc.ap()[:, sl], tmpc.ap()[:, sl],
                                    tmc2.ap()[:, sl], Alu.subtract).then_inc(sem["s_std"], 1)
                    v.wait_ge(sem["s_sts"], g + 1)
                    mzp(v.reciprocal(invc.ap()[:, sl], invc.ap()[:, sl]))
                    mzf()
                    mzp(v.tensor_tensor(tmc2.ap()[:, sl], tmpc.ap()[:, sl],
                                        invc.ap()[:, sl], Alu.mult))
                    mzf()
                    v.tensor_tensor(tmc2.ap()[:, sl], tmc2.ap()[:, sl],
                                    invc.ap()[:, sl], Alu.mult).then_inc(sem["s_std2"], 1)
                    v.wait_ge(sem["s_sts2"], g + 1)
                    mzp(v.reciprocal(c2c.ap()[:, sl], c2c.ap()[:, sl]))
                    mzf()
                    mzp(v.tensor_tensor(c2c.ap()[:, sl], invc.ap()[:, sl],
                                        c2c.ap()[:, sl], Alu.mult))
                    v.scalar_tensor_tensor(nm1c.ap()[:, sl], muc.ap()[:, sl],
                                           -1.0, invc.ap()[:, sl], Alu.mult,
                                           Alu.mult)
                    mzf()
                    mzp(v.scalar_tensor_tensor(nm2c.ap()[:, sl],
                                               muc.ap()[:, sl],
                                               -1.0, c2c.ap()[:, sl], Alu.mult,
                                               Alu.mult))
                    mzf()
                    v.sem_inc(sem["s_stats"], 1)
                    for t in range(4 * g, 4 * g + 4):
                        if t >= 2 and TRUNC >= 2:
                            v.wait_ge(sem[f"s_xTp{(t - 2) // 2}"], 32)
                        v.tensor_scalar(xnbq.ap()[:, t % 2, :],
                                        xbuf.ap()[:, t, :],
                                        invc.ap()[:, t:t + 1],
                                        nm1c.ap()[:, t:t + 1], Alu.mult,
                                        Alu.add).then_inc(sem["s_xnb"], 1)
                if TRUNC >= 3:
                    v.wait_ge(sem["s_aw"], KC)
                    v.reduce_sum(col1.ap(), awc.ap(), AX.X).then_inc(sem["s_dcol"], 1)
                for g in range(4 if TRUNC >= 3 else 0):
                    if g == 1:
                        v.wait_ge(sem["s_b4"], 16)
                        mzp(v.reduce_sum(sc11.ap()[:, 11:12], row128.ap(),
                                         AX.X))
                        mzf()
                        v.tensor_scalar(sc11.ap()[:, 4:5], sc11.ap()[:, 11:12],
                                        float(0.7 / (D * D)), None, Alu.mult)
                        mzp(v.tensor_scalar(sc11.ap()[:, 5:6],
                                            sc11.ap()[:, 11:12],
                                            float(-0.7 / (D * D)), None,
                                            Alu.mult))
                        mzf()
                        v.sem_inc(sem["s_b5"], 1)
                    for rt in range(4 * g, 4 * g + 4):
                        v.wait_ge(sem["s_xn2"], rt + 1)
                        v.reduce_max(lmax.ap()[:, rt:rt + 1],
                                     xn2r.ap()[:, rt % 2, :], AX.X,
                                     apply_absolute_value=True).then_inc(sem["s_lm"], 1)
                        st = rt - 2
                        if st >= 0 and TRUNC >= 4 and T4GP:
                            v.wait_ge(sem["s_p2"], st + 1)
                            v.tensor_tensor(xbuf.ap()[:, st, :],
                                            xbuf.ap()[:, st, :],
                                            gmr.ap()[:, st % 2, :],
                                            Alu.add).then_inc(sem["s_s1"], 1)
                        if rt >= 13 and TRUNC >= 6:
                            c = rt - 13
                            v.wait_ge(sem["s_sg"], c + 1)
                            v.tensor_tensor(mTb.ap()[:, c, :],
                                            sgn2.ap()[:, 0, :],
                                            sgn2.ap()[:, 1, :],
                                            Alu.add).then_inc(sem["s_m"], 1)
                    if TRUNC < 4 or not T4AB:
                        continue
                    sl2 = slice(g * 512, (g + 1) * 512)
                    v.wait_ge(sem["s_abe"], g + 1)
                    if g >= 1:
                        v.wait_ge(sem["s_hsb"], g)
                    ih = 0.0 if g == 0 else hscr.ap()[:, 511:512]
                    ip = 1.0 if g == 0 else pscr.ap()[:, 511:512]
                    v.tensor_tensor_scan(hscr.ap(), a_bf.ap()[:, sl2],
                                         b_bf.ap()[:, sl2], ih, Alu.mult,
                                         Alu.add)
                    v.tensor_tensor_scan(pscr.ap(), a_bf.ap()[:, sl2],
                                         a_bf.ap()[:, sl2], ip, Alu.mult,
                                         Alu.bypass).then_inc(sem["s_scan"], 1)
                for st in ((NT - 2, NT - 1) if TRUNC >= 4 and T4GP else ()):
                    v.wait_ge(sem["s_p2"], st + 1)
                    v.tensor_tensor(xbuf.ap()[:, st, :], xbuf.ap()[:, st, :],
                                    gmr.ap()[:, st % 2, :],
                                    Alu.add).then_inc(sem["s_s1"], 1)
                if TRUNC >= 6:
                    v.wait_ge(sem["s_lm"], 16)
                    v.reduce_max(lmax1.ap(), lmax.ap(), AX.X).then_inc(sem["s_b1"], 1)
                    v.wait_ge(sem["s_b2"], 16)
                    v.reduce_max(sc11.ap()[:, 0:1], row128.ap(), AX.X).then_inc(sem["s_b3"], 1)
                    for c in range(3, KC):
                        v.wait_ge(sem["s_sg"], c + 1)
                        v.tensor_tensor(mTb.ap()[:, c, :], sgn2.ap()[:, 0, :],
                                        sgn2.ap()[:, 1, :],
                                        Alu.add).then_inc(sem["s_m"], 1)
                if TRUNC >= 5:
                    v.wait_ge(sem["s_sum"], 1)
                    v.tensor_scalar(wchb.ap(), wcTb.ap(), hinc.ap(), None,
                                    Alu.mult).then_inc(sem["s_wch"], 1)
                def sinv_chain():
                    v.wait_ge(sem["s_gm2"], 1)
                    mzp(v.reciprocal(sc11.ap()[:, 1:2], sc11.ap()[:, 0:1]))
                    mzf()
                    v.tensor_scalar(sc11.ap()[:, 2:3], sc11.ap()[:, 1:2], 127.0,
                                    None, Alu.mult)
                    mzp(v.tensor_scalar(sc11.ap()[:, 3:4], sc11.ap()[:, 0:1],
                                        float(np.float32(1.0)
                                              / np.float32(127.0)),
                                        None, Alu.mult))
                    mzf()
                    v.sem_inc(sem["s_sv1"], 1)

                def alpha_part1():
                    v.wait_ge(sem["s_sg"], KC)
                    mzp(v.tensor_tensor(nmc.ap(), nmc.ap(), sqc.ap(), Alu.add))
                    mzf()
                    v.reduce_sum(col1.ap(), nmc.ap(), AX.X).then_inc(sem["s_acol"], 1)

                def alpha_part2():
                    v.wait_ge(sem["s_b6"], 16)
                    v.reduce_sum(sc11.ap()[:, 4:5], row128.ap(), AX.X)
                    v.wait_ge(sem["s_sq2"], KC)
                    v.reduce_sum(col1.ap(), awc.ap(), AX.X).then_inc(sem["s_acol"], 2)

                def alpha_part3():
                    v.wait_ge(sem["s_b6"], 32)
                    mzp(v.reduce_sum(sc11.ap()[:, 5:6], row128.ap(), AX.X))
                    mzf()
                    mzp(v.tensor_tensor(sc11.ap()[:, 6:7], sc11.ap()[:, 5:6],
                                        deltaP.ap()[0:1, 0:1], Alu.mult))
                    mzf()
                    mzp(v.tensor_tensor(sc11.ap()[:, 6:7], sc11.ap()[:, 6:7],
                                        sc11.ap()[:, 4:5], Alu.add))
                    mzf()
                    mzp(v.tensor_scalar_max(sc11.ap()[:, 7:8],
                                            sc11.ap()[:, 5:6], 1.0))
                    mzf()
                    mzp(v.reciprocal(sc11.ap()[:, 7:8], sc11.ap()[:, 7:8]))
                    mzf()
                    mzp(v.tensor_tensor(sc11.ap()[:, 8:9], sc11.ap()[:, 6:7],
                                        sc11.ap()[:, 7:8], Alu.mult))
                    mzf()
                    mzp(v.tensor_tensor(sc11.ap()[:, 9:10], sc11.ap()[:, 8:9],
                                        sc11.ap()[:, 3:4], Alu.mult))
                    mzf()
                    mzp(v.tensor_scalar(sc11.ap()[:, 10:11], sc11.ap()[:, 9:10],
                                        0.5, None, Alu.mult))
                    mzf()
                    v.sem_inc(sem["s_alpha"], 1)

                def round_q(rt):
                    if rt < 2:
                        v.wait_ge(sem[f"s_xTp{7}"], 32)
                        v.wait_ge(sem["s_sq2"], KC)
                    v.wait_ge(sem[f"s_rlk{rt // 2}"], 16)
                    v.wait_ge(sem["s_sinv"], 16)
                    if rt >= 2:
                        v.wait_ge(sem[f"s_qTp{(rt - 2) // 2}"], 32)
                    v.tensor_scalar(xn2r.ap()[:, rt % 2, :],
                                    xn2r.ap()[:, rt % 2, :], sinvc.ap(),
                                    magicc.ap(), Alu.mult, Alu.add)
                    v.tensor_scalar(xnbq.ap()[:, rt % 2, :],
                                    xn2r.ap()[:, rt % 2, :], -MAGIC, None,
                                    Alu.add).then_inc(sem["s_q"], 1)

                for tt in range(NT if TRUNC >= 5 else 0):
                    for h in (0, 1):
                        p = 2 * tt + h
                        ubuf = uP.ap() if p % 2 == 0 else dxP.ap()[:, 0:512]
                        v.wait_ge(sem["s_u"], p + 1)
                        v.wait_ge(sem["s_g"], tt + 1)
                        v.scalar_tensor_tensor(
                            gview(tt)[:, h * 512:(h + 1) * 512], ubuf, 1.0,
                            gview(tt)[:, h * 512:(h + 1) * 512], Alu.mult,
                            Alu.mult).then_inc(sem["s_p1"], 1)
                    v.wait_ge(sem["s_s1"], tt + 1)
                    if DEBUG_DUMPS:
                        v.wait_ge(sem["s_dbgA"], 16 * NT)
                    v.tensor_tensor(xbuf.ap()[:, tt, :], xbuf.ap()[:, tt, :],
                                    gview(tt), Alu.add).then_inc(sem["s_s2"], 1)
                    if TRUNC < 6:
                        continue
                    if tt == 2:
                        alpha_part1()
                    if tt == 3:
                        sinv_chain()
                    if tt == 4:
                        alpha_part2()
                    if tt == 6:
                        alpha_part3()
                    rt = tt - 5
                    if rt >= 0:
                        round_q(rt)
                if TRUNC >= 6:
                    for rt in range(NT - 5, NT):
                        round_q(rt)
                for tt in range(NT if TRUNC >= 6 else 0):
                    v.wait_ge(sem["s_ffn"], tt + 1)
                    v.wait_ge(sem["s_cb"], 16)
                    if DEBUG_DUMPS:
                        v.wait_ge(sem["s_dbgB"], 16 * NT)
                    par = [zgA, zgB][tt % 2]
                    v.scalar_tensor_tensor(xbuf.ap()[:, tt, :], par.ap(),
                                           cbc.ap(), xbuf.ap()[:, tt, :],
                                           Alu.mult, Alu.add).then_inc(sem["s_out"], 1)

            @blk.gpsimd
            def _(gp):
                nio = [0]

                def gdma(out, in_):
                    gp.dma_start(out=out, in_=in_).then_inc(sem["s_io"], 16)
                    nio[0] += 16
                    gp.wait_ge(sem["s_io"], nio[0])

                for kc in range(KC):
                    gp.dma_start(out=wabTb.ap()[:, kc, :],
                                 in_=wabT_d[kc * 128:(kc + 1) * 128, :]).then_inc(sem["s_w"], 16)
                gp.dma_start(out=wcTb.ap(), in_=wcT_d).then_inc(sem["s_w"], 16)
                gp.dma_start(out=bDb.ap(), in_=bD_d).then_inc(sem["s_w"], 16)
                gp.dma_start(out=bgb.ap(), in_=bg_d).then_inc(sem["s_w"], 16)
                gp.dma_start(out=myoff_s.ap(), in_=myoff_d).then_inc(sem["s_w"], 16)
                gp.dma_start(out=partoff_s.ap(), in_=partoff_d).then_inc(sem["s_w"], 16)
                for kc in range(KC):
                    gp.dma_start(out=wgTb.ap()[:, kc, :],
                                 in_=wgT_d[kc * 128:(kc + 1) * 128, :]).then_inc(sem["s_w"], 16)
                    gp.dma_start(out=wdTb.ap()[:, kc, :],
                                 in_=wdT_d[kc * 128:(kc + 1) * 128, :]).then_inc(sem["s_w"], 16)
                for k in range(KC if TRUNC >= 1 else 0):
                    if k >= 1:
                        gp.wait_ge(sem["s_aw"], k)
                    gp.dma_start(out=wfr.ap()[:, 0, :],
                                 in_=wfT_d[k * 128:(k + 1) * 128, :]).then_inc(sem["s_wf1"], 16)
                moff = None
                if TRUNC >= 5:
                    gp.wait_ge(sem["s_mz"], 1)
                    gp.wait_ge(sem["s_w"], 16 * 29)
                    gdma(cc1_in.ap(), z840.ap())
                    moff = gp.value_load(myoff_s.ap())
                for tt in range(NT if TRUNC >= 3 else 0):
                    pt = tt - 2
                    if 0 <= pt and TRUNC >= 4 and T4GP:
                        gp.wait_ge(sem["s_gm"], pt + 1)
                        gp.wait_ge(sem["s_dxe"], pt + 1)
                        gp.tensor_tensor(gmr.ap()[:, pt % 2, :],
                                         gmr.ap()[:, pt % 2, :],
                                         dxe.ap()[:, pt % 2, :],
                                         Alu.mult).then_inc(sem["s_p2"], 1)
                    gp.wait_ge(sem["s_z"], tt + 1)
                    gp.wait_ge(sem["s_gb"], 32)
                    gp.tensor_tensor(xn2r.ap()[:, tt % 2, :],
                                     xn2r.ap()[:, tt % 2, :], gammaB.ap(),
                                     Alu.mult)
                    gp.tensor_tensor(xn2r.ap()[:, tt % 2, :],
                                     xn2r.ap()[:, tt % 2, :], betaB.ap(),
                                     Alu.add).then_inc(sem["s_xn2"], 1)
                    if tt == 2:
                        gp.wait_ge(sem["s_dcol"], 1)
                        gdma(scrA.ap().rearrange("(p one) -> p one", one=1),
                             col1.ap())
                        gp.dma_start(out=row128.ap(),
                                     in_=scrA.ap().rearrange(
                                         "(one p) -> one p", one=1)).then_inc(sem["s_b4"], 16)
                    if tt == 4:
                        gp.wait_ge(sem["s_b5"], 1)
                        gp.dma_start(out=scrB.ap()[4:6].rearrange("(a b) -> a b", a=1),
                                     in_=sc11.ap()[:, 4:6]).then_inc(sem["s_dscr"], 16)
                    if 12 <= tt and TRUNC >= 6:
                        k = tt - 12
                        gp.wait_ge(sem["s_aw"], KC)
                        if k >= 2:
                            gp.wait_ge(sem["s_sg"], k - 1)
                        gp.dma_start(out=wfr.ap()[:, k % 2, :],
                                     in_=wfT_d[k * 128:(k + 1) * 128, :]).then_inc(
                                         sem["s_wf2" if k % 2 == 0 else "s_wf1"], 16)
                if TRUNC >= 5:
                    gp.wait_ge(sem["s_scan"], 4)
                    gdma(scrH.ap().rearrange("(p one) -> p one", one=1),
                         hscr.ap()[:, 511:512])
                    gdma(cc1_in.ap().rearrange("r c -> (r c)")[bass.ds(moff, 16)],
                         scrH.ap())
                    gp.collective_compute(
                        "AllReduce", Alu.add,
                        replica_groups=[list(range(NCORE))],
                        ins=[cc1_in.ap().opt()],
                        outs=[cc1_out.ap().opt()]).then_inc(sem["s_cc1"], 1)
                for pt in ((NT - 2, NT - 1) if TRUNC >= 4 and T4GP else ()):
                    gp.wait_ge(sem["s_gm"], pt + 1)
                    gp.wait_ge(sem["s_dxe"], pt + 1)
                    gp.tensor_tensor(gmr.ap()[:, pt % 2, :],
                                     gmr.ap()[:, pt % 2, :],
                                     dxe.ap()[:, pt % 2, :],
                                     Alu.mult).then_inc(sem["s_p2"], 1)
                if TRUNC >= 6:
                    gp.wait_ge(sem["s_b1"], 1)
                    gp.wait_ge(sem["s_b5"], 1)
                    gdma(scrA.ap().rearrange("(p one) -> p one", one=1), lmax1.ap())
                    gp.dma_start(out=row128.ap(),
                                 in_=scrA.ap().rearrange("(one p) -> one p",
                                                         one=1)).then_inc(sem["s_b2"], 16)
                    gp.wait_ge(sem["s_b3"], 1)
                    gp.dma_start(out=scrB.ap()[0:1].rearrange("(a b) -> a b", a=1),
                                 in_=sc11.ap()[:, 0:1]).then_inc(sem["s_c2s"], 16)
                    gp.wait_ge(sem["s_c2b"], 16)
                    gp.collective_compute(
                        "AllReduce", Alu.max,
                        replica_groups=[list(range(NCORE))],
                        ins=[cc2_in.ap().opt()],
                        outs=[cc2_out.ap().opt()]).then_inc(sem["s_cc2"], 1)
                if TRUNC >= 6:
                    for k in range(4, KC):
                        gp.wait_ge(sem["s_sg"], k - 1)
                        gp.dma_start(out=wfr.ap()[:, k % 2, :],
                                     in_=wfT_d[k * 128:(k + 1) * 128, :]).then_inc(
                                         sem["s_wf2" if k % 2 == 0 else "s_wf1"], 16)
                if TRUNC >= 6:
                    for j in range(KC):
                        gp.wait_ge(sem[f"s_spk{j}"], 16)
                    gp.dma_start(
                        out=xn2r.ap()[:, 0:2, :],
                        in_=xn2_d[0:256, :].rearrange(
                            "(s p) d -> p s d", s=2)).then_inc(sem["s_rlk0"], 16)
                if TRUNC >= 5:
                    gp.wait_ge(sem["s_cc1"], 1)
                    poff = gp.value_load(partoff_s.ap())
                    gdma(scrH2.ap(),
                         cc1_out.ap().rearrange("r c -> (r c)")[bass.ds(poff, 16)])
                    gdma(hinc.ap(), scrH2.ap().rearrange("(p one) -> p one", one=1))
                    gp.sem_inc(sem["s_sum"], 1)
                if TRUNC >= 6:
                    gp.wait_ge(sem["s_cc2"], 1)
                    gdma(gath2.ap(), cc2_out.ap())
                    gp.tensor_scalar_max(sc11.ap()[:, 0:1], gath2.ap()[0:1, 0:1],
                                         1e-8).then_inc(sem["s_gm2"], 1)
                    gp.wait_ge(sem["s_sv1"], 1)
                    gp.dma_start(out=scrB.ap()[1:2].rearrange("(a b) -> a b", a=1),
                                 in_=sc11.ap()[:, 2:3]).then_inc(sem["s_svs"], 16)
                    gp.wait_ge(sem["s_acol"], 1)
                    gdma(scrA.ap().rearrange("(p one) -> p one", one=1), col1.ap())
                    gp.dma_start(out=row128.ap(),
                                 in_=scrA.ap().rearrange("(one p) -> one p",
                                                         one=1)).then_inc(sem["s_b6"], 16)
                    gp.wait_ge(sem["s_acol"], 3)
                    gdma(scrA.ap().rearrange("(p one) -> p one", one=1), col1.ap())
                    gp.dma_start(out=row128.ap(),
                                 in_=scrA.ap().rearrange("(one p) -> one p",
                                                         one=1)).then_inc(sem["s_b6"], 16)
                    gp.wait_ge(sem["s_alpha"], 1)
                    gp.dma_start(out=scrB.ap()[2:3].rearrange("(a b) -> a b", a=1),
                                 in_=sc11.ap()[:, 10:11]).then_inc(sem["s_cbs"], 16)
                    for k in range(1, KC):
                        gp.wait_ge(sem["s_q"], 2 * k)
                        gp.dma_start(
                            out=xn2r.ap()[:, 0:2, :],
                            in_=xn2_d[k * 256:(k + 1) * 256, :].rearrange(
                                "(s p) d -> p s d", s=2)).then_inc(sem[f"s_rlk{k}"], 16)

            @blk.tensor
            def _(t):
                if TRUNC >= 4:
                    t.wait_ge(sem["s_w"], 16 * 29)
                    t.wait_ge(sem["s_mz"], 1)
                for tt in range(NT if TRUNC >= 4 else 0):
                    g = tt // 4
                    par = [zgA, zgB][tt % 2]
                    t.wait_ge(sem[f"s_xTp{tt // 2}"], 32)
                    if tt >= 2 and T4G and T4G_G:
                        t.wait_ge(sem["s_g"], tt - 1)
                    if tt >= 2 and T4G and T4G_M:
                        t.wait_ge(sem["s_gm"], tt - 1)
                    if tt >= 1 and T4G and T4G_D:
                        t.wait_ge(sem["s_dxe"], tt)
                    for kc in range(KC):
                        lhs = xnTq.ap()[:, tt, kc, :]
                        for f in (0, 1):
                            fsl = slice(f * 512, (f + 1) * 512)
                            t.matmul(par.ap()[:, fsl], lhs,
                                     wgTb.ap()[:, kc, fsl],
                                     start=(kc == 0), stop=False)
                        for f in (0, 1):
                            fsl = slice(f * 512, (f + 1) * 512)
                            t.matmul(dxP.ap()[:, fsl], lhs,
                                     wdTb.ap()[:, kc, fsl],
                                     start=(kc == 0), stop=False)
                    for f in (0, 1):
                        fsl = slice(f * 512, (f + 1) * 512)
                        mm = t.matmul(par.ap()[:, fsl], onesb.ap(),
                                      bgb.ap()[:, fsl], start=False, stop=True)
                        if f == 1:
                            mm.then_inc(sem["s_zg"], 1)
                    for f in (0, 1):
                        fsl = slice(f * 512, (f + 1) * 512)
                        mm = t.matmul(dxP.ap()[:, fsl], onesb.ap(),
                                      bDb.ap()[:, fsl], start=False, stop=True)
                        if f == 1:
                            mm.then_inc(sem["s_dx"], 1)
                    if tt % 4 == 1:
                        t.wait_ge(sem[f"s_xTp{2 * g + 1}"], 32)
                        if g >= 1 and T4AB:
                            t.wait_ge(sem["s_abe"], g)
                        for kc in range(KC):
                            mm = t.matmul(abP.ap(), wabTb.ap()[:, kc, :],
                                          xnTq.ap()[:, 4 * g:4 * g + 4, kc, :],
                                          start=(kc == 0), stop=(kc == KC - 1))
                            if kc == KC - 1:
                                mm.then_inc(sem["s_ab"], 1)
                if TRUNC >= 5:
                    t.wait_ge(sem["s_wch"], 1)
                    t.wait_ge(sem["s_hsb"], 4)
                    t.wait_ge(sem["s_dxe"], NT)
                for ut in range(NT if TRUNC >= 5 else 0):
                    for h in (0, 1):
                        fsl = slice(h * 512, (h + 1) * 512)
                        p = 2 * ut + h
                        ubuf = uP.ap() if p % 2 == 0 else dxP.ap()[:, 0:512]
                        if p >= 2:
                            t.wait_ge(sem["s_p1"], p - 1)
                        t.matmul(ubuf,
                                 hsb.ap()[:, ut * 128:(ut + 1) * 128],
                                 wcTb.ap()[:, fsl], start=True, stop=False)
                        t.matmul(ubuf,
                                 Pb.ap()[:, ut * 128:(ut + 1) * 128],
                                 wchb.ap()[:, fsl], start=False,
                                 stop=True).then_inc(sem["s_u"], 1)
                if TRUNC >= 6:
                    t.wait_ge(sem["s_m"], KC)
                for tt in range(NT if TRUNC >= 6 else 0):
                    par = [zgA, zgB][tt % 2]
                    t.wait_ge(sem[f"s_qTp{tt // 2}"], 32)
                    if tt < 2:
                        t.wait_ge(sem["s_g"], NT)
                        t.wait_ge(sem["s_gm"], NT)
                    else:
                        t.wait_ge(sem["s_out"], tt - 1)
                    for kc in range(KC):
                        lhs = xnTq.ap()[:, tt, kc, :]
                        for f in (0, 1):
                            fsl = slice(f * 512, (f + 1) * 512)
                            mm = t.matmul(par.ap()[:, fsl], lhs,
                                          mTb.ap()[:, kc, fsl],
                                          start=(kc == 0), stop=(kc == KC - 1))
                            if kc == KC - 1 and f == 1:
                                mm.then_inc(sem["s_ffn"], 1)

    return nc



_CACHED = {}


def kernel(L0_attn_x, L0_attn_WA, L0_attn_WB, L0_attn_WC, L0_attn_WD,
           L0_attn_bD, L0_attn_Wg, L0_attn_bg, L0_ffn_weight,
           L0_ffn_gamma, L0_ffn_beta):
    x = np.ascontiguousarray(np.asarray(L0_attn_x, np.float32))
    B, Sq, D_ = x.shape
    wgT = np.ascontiguousarray(np.asarray(L0_attn_Wg, np.float32).T)
    wdT = np.ascontiguousarray(np.asarray(L0_attn_WD, np.float32).T)
    wfT = np.ascontiguousarray(np.asarray(L0_ffn_weight, np.float32).T)
    wabT = np.ascontiguousarray(
        np.concatenate([np.asarray(L0_attn_WA, np.float32),
                        np.zeros((16, D), np.float32),
                        np.asarray(L0_attn_WB, np.float32)], axis=0).T)
    wcT = np.ascontiguousarray(np.asarray(L0_attn_WC, np.float32).T)
    gamma = np.ascontiguousarray(np.asarray(L0_ffn_gamma, np.float32).reshape(1, D))
    beta = np.ascontiguousarray(np.asarray(L0_ffn_beta, np.float32).reshape(1, D))
    bD = np.ascontiguousarray(np.asarray(L0_attn_bD, np.float32).reshape(1, D))
    bg = np.ascontiguousarray(np.asarray(L0_attn_bg, np.float32).reshape(1, D))

    if "nc" not in _CACHED:
        _CACHED["nc"] = build_nc()
    nc = _CACHED["nc"]

    in_maps = []
    for c in range(NCORE):
        b, h = c // 2, c % 2
        in_maps.append(dict(
            x=np.ascontiguousarray(x[b, h * TC:(h + 1) * TC, :]),
            wgT=wgT, wdT=wdT, wfT=wfT, wabT=wabT, wcT=wcT,
            gamma=gamma, beta=beta, bD=bD, bg=bg,
            myoff=np.array([[c * 40 + 2]], dtype=np.int32),
            partoff=np.array([[(c - 1) * 40 + 2 if h == 1 else 20]],
                             dtype=np.int32),
        ))
    res = run_bass_kernel_spmd(nc, in_maps, core_ids=list(range(NCORE)))
    out = np.empty((B, Sq, D), np.float32)
    for c in range(NCORE):
        b, h = c // 2, c % 2
        out[b, h * TC:(h + 1) * TC, :] = res.results[c]["out"]
    return out


# version 16
# speedup vs baseline: 1.0232x; 1.0102x over previous
import contextlib
import numpy as np
import concourse.bass as bass
import concourse.mybir as mybir
from concourse.bass_utils import run_bass_kernel_spmd

F32 = mybir.dt.float32
BF16 = mybir.dt.bfloat16
I32 = mybir.dt.int32
Alu = mybir.AluOpType
Act = mybir.ActivationFunctionType
AX = mybir.AxisListType

NCORE = 8
DEBUG_DUMPS = False
TRUNC = 99
T4TE = True
T4AB = True
T4G = True
T4GP = True
T4GV = True
T4G_G = True
T4G_M = True
T4G_D = True
RACECHECK = False
D = 1024
N = 16
TC = 2048
NT = 16
KC = 8
MAGIC = 12582912.0
RD = float(np.float32(1.0 / 1024.0))


def build_nc():
    nc = bass.Bass("TRN2", target_bir_lowering=False,
                   detect_race_conditions=RACECHECK)

    def din(name, shape, dt=F32):
        return nc.dram_tensor(name, shape, dt, kind="ExternalInput").ap()

    x_d = din("x", [TC, D])
    wgT_d = din("wgT", [D, D])
    wdT_d = din("wdT", [D, D])
    wfT_d = din("wfT", [D, D])
    wabT_d = din("wabT", [D, 3 * N])
    wcT_d = din("wcT", [N, D])
    gamma_d = din("gamma", [1, D])
    beta_d = din("beta", [1, D])
    bD_d = din("bD", [1, D])
    bg_d = din("bg", [1, D])
    myoff_d = din("myoff", [1, 1], I32)
    partoff_d = din("partoff", [1, 1], I32)
    out_d = nc.dram_tensor("out", [TC, D], F32, kind="ExternalOutput").ap()
    if DEBUG_DUMPS:
        dbgA_d = nc.dram_tensor("dbgA", [TC, D], F32, kind="ExternalOutput").ap()
        dbgB_d = nc.dram_tensor("dbgB", [TC, D], F32, kind="ExternalOutput").ap()
        dbgS_d = nc.dram_tensor("dbgS", [1, 16], F32, kind="ExternalOutput").ap()
        dbgM_d = nc.dram_tensor("dbgM", [128, KC * D], BF16,
                                kind="ExternalOutput").ap()
        dbgQ_d = nc.dram_tensor("dbgQ", [128, NT * KC * 128], BF16,
                                kind="ExternalOutput").ap()

    xn2_d = nc.dram_tensor("xn2d", [TC, D], F32).ap()
    cc1_in = nc.dram_tensor("cc1_in", [8, 40], F32)
    cc1_out = nc.dram_tensor("cc1_out", [8, 40], F32)
    cc2_in = nc.dram_tensor("cc2_in", [8, 4], F32)
    cc2_out = nc.dram_tensor("cc2_out", [8, 4], F32)
    scrA = nc.dram_tensor("scrA", [128], F32)
    scrH = nc.dram_tensor("scrH", [16], F32)
    scrH2 = nc.dram_tensor("scrH2", [16], F32)
    scrB = nc.dram_tensor("scrB", [8], F32)

    S = lambda *a, **k: nc.alloc_sbuf_tensor(*a, **k)
    xbuf = S("xbuf", [128, NT, D], F32)
    xnTq = S("xnTq", [128, NT, KC, 128], BF16, is_transpose_dest=True)
    wgTb = S("wgTb", [128, KC, D], BF16)
    wdTb = S("wdTb", [128, KC, D], BF16)
    mTb = S("mTb", [128, KC, D], BF16)
    wcTb = S("wcTb", [N, D], BF16)
    wchb = S("wchb", [N, D], BF16)
    wabTb = S("wabTb", [128, KC, 3 * N], BF16)
    onesb = S("onesb", [1, 128], BF16)
    bDb = S("bDb", [1, D], BF16)
    bgb = S("bgb", [1, D], BF16)
    gammaB = S("gammaB", [128, D], F32)
    betaB = S("betaB", [128, D], F32)
    xn2r = S("xn2r", [128, 2, D], F32)
    xnbq = S("xnbq", [128, 2, D], BF16)
    gmr = S("gmr", [128, 2, D], BF16)
    dxe = S("dxe", [128, 2, D], BF16)
    wfr = S("wfr", [128, 2, D], F32)
    sgn2 = S("sgn2", [128, 2, D], BF16)
    a_bf = S("a_bf", [N, TC], BF16)
    b_bf = S("b_bf", [N, TC], BF16)
    hsb = a_bf
    Pb = b_bf
    hscr = S("hscr", [N, 512], F32)
    pscr = S("pscr", [N, 512], F32)
    sumc = S("sumc", [128, NT], F32)
    ssc = S("ssc", [128, NT], F32)
    muc = S("muc", [128, NT], F32)
    invc = S("invc", [128, NT], F32)
    c2c = S("c2c", [128, NT], F32)
    nm1c = S("nm1c", [128, NT], F32)
    nm2c = S("nm2c", [128, NT], F32)
    tmpc = S("tmpc", [128, NT], F32)
    tmc2 = S("tmc2", [128, NT], F32)
    lmax = S("lmax", [128, NT], F32)
    lmax1 = S("lmax1", [128, 1], F32)
    awc = S("awc", [128, KC], F32)
    nmc = S("nmc", [128, KC], F32)
    sqc = S("sqc", [128, KC], F32)
    class _ColView:
        def __init__(self, t, j):
            self._t, self._j = t, j

        def ap(self):
            return self._t.ap()[:, self._j:self._j + 1]

    colp = S("colp", [128, 8], F32)
    col1 = _ColView(colp, 0)
    deltaP = _ColView(colp, 1)
    deltaN = _ColView(colp, 2)
    sinvc = _ColView(colp, 3)
    cbc = _ColView(colp, 4)
    magicc = _ColView(colp, 5)
    epsc = _ColView(colp, 6)
    row128 = S("row128", [1, 128], F32)
    z840 = S("z840", [8, 40], F32)
    gath2 = S("gath2", [8, 4], F32)
    hinc = S("hinc", [N, 1], F32)
    sc11 = S("sc11", [1, 12], F32)
    myoff_s = S("myoff_s", [1, 1], I32)
    partoff_s = S("partoff_s", [1, 1], I32)

    zgA = nc.alloc_psum_tensor("zgA", [128, D], F32)
    zgB = nc.alloc_psum_tensor("zgB", [128, D], F32)
    dxP = nc.alloc_psum_tensor("dxP", [128, D], F32)
    uP = nc.alloc_psum_tensor("uP", [128, 512], F32)
    abP = nc.alloc_psum_tensor("abP", [3 * N, 512], F32)


    def gview(tt):
        return xnTq.ap()[:, tt, :, :].rearrange("p k f -> p (k f)")

    ctx = contextlib.ExitStack()
    sem = {}
    for name in (["s_gb", "s_w", "s_ssq", "s_std", "s_sts", "s_std2",
                  "s_sts2", "s_stats", "s_xnb", "s_z",
                  "s_xn2", "s_lm", "s_ab", "s_abe", "s_scan",
                  "s_hsb", "s_zg", "s_dx", "s_u", "s_g", "s_gm", "s_p2",
                  "s_s1", "s_p1", "s_s2", "s_cc1", "s_wch", "s_cc2", "s_q",
                  "s_ffn", "s_out", "s_st", "s_mz",
                  "s_aw", "s_dcol", "s_b1", "s_b2", "s_b3", "s_b4",
                  "s_b5", "s_b6", "s_delta", "s_sg", "s_m",
                  "s_num", "s_sq2", "s_acol", "s_alpha", "s_gm2", "s_sv1",
                  "s_sinv", "s_cb", "s_io", "s_dscr", "s_c2s", "s_c2b",
                  "s_svs", "s_cbs", "s_io2", "s_sum", "s_dxe", "s_dbgA",
                  "s_dbgB"]
                 + ["s_wf1", "s_wf2"]
                 + [f"s_xg{i}" for i in range(4)]
                 + [f"s_xTp{i}" for i in range(NT // 2)]
                 + [f"s_qTp{i}" for i in range(NT // 2)]
                 + [f"s_spk{i}" for i in range(KC)]
                 + [f"s_rlk{i}" for i in range(KC)]):
        sem[name] = ctx.enter_context(nc.semaphore(name))

    with ctx:
        with nc.Block() as blk:

            @blk.sync
            def _(sync):
                if TRUNC >= 2:
                    for tt in range(NT):
                        sync.wait_ge(sem["s_xnb"], tt + 1)
                        sync.dma_start(
                            out=xnTq.ap()[:, tt, :, :],
                            in_=xnbq.ap()[:, tt % 2, :].rearrange(
                                "p (k f) -> p k f", k=KC),
                            transpose=True).then_inc(sem[f"s_xTp{tt // 2}"], 16)
                if TRUNC >= 3:
                    for k in range(KC):
                        if k == 4:
                            sync.wait_ge(sem["s_dscr"], 16)
                            sync.dma_start(out=deltaP.ap(),
                                           in_=scrB.ap()[4:5].rearrange(
                                               "(a b) -> a b", a=1)
                                           .broadcast_to([128, 1])).then_inc(sem["s_delta"], 16)
                            sync.dma_start(out=deltaN.ap(),
                                           in_=scrB.ap()[5:6].rearrange(
                                               "(a b) -> a b", a=1)
                                           .broadcast_to([128, 1])).then_inc(sem["s_delta"], 16)
                        sync.wait_ge(sem["s_xn2"], 2 * k + 2)
                        sl2 = slice(0, 2)
                        sync.dma_start(
                            out=xn2_d[k * 256:(k + 1) * 256, :].rearrange(
                                "(s p) d -> p s d", s=2),
                            in_=xn2r.ap()[:, sl2, :]).then_inc(sem[f"s_spk{k}"], 16)
                if TRUNC >= 6:
                    sync.wait_ge(sem["s_c2s"], 16)
                    sync.dma_start(out=cc2_in.ap(),
                                   in_=z840.ap()[0:8, 0:4]).then_inc(sem["s_io2"], 16)
                    sync.wait_ge(sem["s_io2"], 16)
                    sync.dma_start(out=cc2_in.ap()[:, 0:1],
                                   in_=scrB.ap()[0:1].rearrange("(a b) -> a b", a=1)
                                   .broadcast_to([8, 1]),
                                   allow_slow_non_contiguous=True).then_inc(sem["s_c2b"], 16)
                if DEBUG_DUMPS:
                    for tt in range(NT):
                        sync.wait_ge(sem["s_s1"], NT)
                        sync.dma_start(out=dbgA_d[tt * 128:(tt + 1) * 128, :],
                                       in_=xbuf.ap()[:, tt, :]).then_inc(sem["s_dbgA"], 16)
                for tt in range(NT if TRUNC >= 6 else 0):
                    sync.wait_ge(sem["s_q"], tt + 1)
                    sync.wait_ge(sem["s_p1"], 2 * tt + 2)
                    sync.wait_ge(sem["s_s2"], tt + 1)
                    sync.dma_start(
                        out=xnTq.ap()[:, tt, :, :],
                        in_=xnbq.ap()[:, tt % 2, :].rearrange(
                            "p (k f) -> p k f", k=KC),
                        transpose=True).then_inc(sem[f"s_qTp{tt // 2}"], 16)
                if DEBUG_DUMPS:
                    for tt in range(NT):
                        sync.wait_ge(sem["s_s2"], NT)
                        sync.dma_start(out=dbgB_d[tt * 128:(tt + 1) * 128, :],
                                       in_=xbuf.ap()[:, tt, :]).then_inc(sem["s_dbgB"], 16)
                if DEBUG_DUMPS:
                    sync.wait_ge(sem["s_cb"], 16)
                    sync.wait_ge(sem["s_sinv"], 16)
                    sync.wait_ge(sem["s_delta"], 32)
                    sync.dma_start(out=dbgS_d[0:1, 0:12],
                                   in_=sc11.ap()).then_inc(sem["s_io2"], 16)
                    sync.dma_start(out=dbgS_d[0:1, 12:13],
                                   in_=deltaP.ap()[0:1, :]).then_inc(sem["s_io2"], 16)
                    sync.dma_start(out=dbgS_d[0:1, 13:14],
                                   in_=sinvc.ap()[0:1, :]).then_inc(sem["s_io2"], 16)
                    sync.dma_start(out=dbgS_d[0:1, 14:15],
                                   in_=cbc.ap()[0:1, :]).then_inc(sem["s_io2"], 16)
                    sync.dma_start(out=dbgS_d[0:1, 15:16],
                                   in_=gath2.ap()[0:1, 0:1]).then_inc(sem["s_io2"], 16)
                    sync.wait_ge(sem["s_io2"], 96)
                for tt in range(NT):
                    if TRUNC >= 6:
                        sync.wait_ge(sem["s_out"], tt + 1)
                    elif TRUNC == 5:
                        sync.wait_ge(sem["s_s2"], tt + 1)
                    elif TRUNC == 4 and T4GP:
                        sync.wait_ge(sem["s_s1"], tt + 1)
                    elif TRUNC == 4:
                        sync.wait_ge(sem["s_dx"], tt + 1)
                    elif TRUNC == 3:
                        sync.wait_ge(sem["s_xn2"], tt + 1)
                    elif TRUNC == 2:
                        sync.wait_ge(sem[f"s_xTp{tt // 2}"], 32)
                    elif TRUNC == 1:
                        sync.wait_ge(sem["s_xnb"], tt + 1)
                    else:
                        sync.wait_ge(sem[f"s_xg{tt // 4}"], 64)
                    sync.dma_start(out=out_d[tt * 128:(tt + 1) * 128, :],
                                   in_=xbuf.ap()[:, tt, :]).then_inc(sem["s_st"], 16)
                sync.wait_ge(sem["s_st"], 16 * NT)
                if DEBUG_DUMPS:
                    sync.dma_start(out=dbgM_d,
                                   in_=mTb.ap().rearrange(
                                       "p k d -> p (k d)")).then_inc(sem["s_io2"], 16)
                    sync.dma_start(out=dbgQ_d,
                                   in_=xnTq.ap().rearrange(
                                       "p a b c -> p (a b c)")).then_inc(sem["s_io2"], 16)
                    sync.wait_ge(sem["s_io2"], 128)

            @blk.scalar
            def _(sc):
                sc.dma_start(out=gammaB.ap(),
                             in_=gamma_d.broadcast_to([128, D])).then_inc(sem["s_gb"], 16)
                sc.dma_start(out=betaB.ap(),
                             in_=beta_d.broadcast_to([128, D])).then_inc(sem["s_gb"], 16)
                for tt in range(NT):
                    sc.dma_start(out=xbuf.ap()[:, tt, :],
                                 in_=x_d[tt * 128:(tt + 1) * 128, :]).then_inc(
                                     sem[f"s_xg{tt // 4}"], 16)
                for g in range(4 if TRUNC >= 1 else 0):
                    sc.wait_ge(sem[f"s_xg{g}"], 64)
                    for t in range(4 * g, 4 * g + 4):
                        if t >= 1:
                            sc.wait_ge(sem["s_ssq"], t)
                        if g >= 1:
                            sc.wait_ge(sem["s_aw"], 2 * g)
                        sc.activation(xn2r.ap()[:, 0, :], xbuf.ap()[:, t, :],
                                      Act.Square, bias=0.0, scale=1.0,
                                      accum_out=ssc.ap()[:, t:t + 1]).then_inc(sem["s_ssq"], 1)
                    sl = slice(4 * g, 4 * g + 4)
                    sc.wait_ge(sem["s_std"], g + 1)
                    sc.activation(invc.ap()[:, sl], tmpc.ap()[:, sl],
                                  Act.Sqrt, bias=epsc.ap(), scale=1.0).then_inc(sem["s_sts"], 1)
                    sc.wait_ge(sem["s_std2"], g + 1)
                    sc.activation(c2c.ap()[:, sl], tmc2.ap()[:, sl],
                                  Act.Sqrt, bias=epsc.ap(), scale=1.0).then_inc(sem["s_sts2"], 1)
                    for k in (2 * g, 2 * g + 1):
                        sc.wait_ge(sem["s_wf1"], 16 * (k + 1))
                        sc.wait_ge(sem["s_ssq"], 4 * g + 4)
                        sc.activation(xn2r.ap()[:, 1, :], wfr.ap()[:, 0, :],
                                      Act.Abs, bias=0.0, scale=1.0,
                                      accum_out=awc.ap()[:, k:k + 1]).then_inc(sem["s_aw"], 1)
                for tt in range(NT + 11 if TRUNC >= 3 else 0):
                    zt = tt
                    if 0 <= zt < NT:
                        sc.wait_ge(sem["s_stats"], zt // 4 + 1)
                        if zt >= 2:
                            sc.wait_ge(sem[f"s_spk{(zt - 2) // 2}"], 16)
                            sc.wait_ge(sem["s_lm"], zt - 1)
                        sc.activation(xn2r.ap()[:, zt % 2, :], xbuf.ap()[:, zt, :],
                                      Act.Identity, bias=nm2c.ap()[:, zt:zt + 1],
                                      scale=c2c.ap()[:, zt:zt + 1]).then_inc(sem["s_z"], 1)
                    et = tt - 2
                    if 0 <= et < NT and TRUNC >= 4 and T4G:
                        par = [zgA, zgB][et % 2]
                        sc.wait_ge(sem["s_dx"], et + 1)
                        if et >= 2 and T4GP:
                            sc.wait_ge(sem["s_s1"], et - 1)
                        if T4G_M:
                            sc.activation(gmr.ap()[:, et % 2, :], par.ap(),
                                          Act.Sigmoid, bias=0.0,
                                          scale=-1.0).then_inc(sem["s_gm"], 1)
                        if T4G_D:
                            sc.copy(dxe.ap()[:, et % 2, 0:512],
                                    dxP.ap()[:, 0:512])
                            sc.copy(dxe.ap()[:, et % 2, 512:1024],
                                    dxP.ap()[:, 512:1024]).then_inc(sem["s_dxe"], 1)
                        if T4G_G:
                            if et % 4 <= 1:
                                sc.wait_ge(sem["s_ab"], et // 4 + 1)
                            gdst = gview(et) if T4GV else xnbq.ap()[:, et % 2, :]
                            sc.activation(gdst, par.ap(), Act.Sigmoid,
                                          bias=0.0, scale=1.0).then_inc(sem["s_g"], 1)
                    if tt % 4 == 0 and 1 <= tt // 4 <= 4 and TRUNC >= 4 and T4AB:
                        g0 = tt // 4 - 1
                        sc.wait_ge(sem["s_ab"], g0 + 1)
                        sc.activation(a_bf.ap()[:, g0 * 512:(g0 + 1) * 512],
                                      abP.ap()[0:N, :], Act.Tanh, bias=0.0,
                                      scale=1.0)
                        sc.copy(b_bf.ap()[:, g0 * 512:(g0 + 1) * 512],
                                abP.ap()[2 * N:3 * N, :]).then_inc(sem["s_abe"], 1)
                    if tt % 4 == 2 and 1 <= (tt - 2) // 4 <= 4 and TRUNC >= 4 and T4AB:
                        g1 = (tt - 2) // 4 - 1
                        sc.wait_ge(sem["s_scan"], g1 + 1)
                        sc.copy(hsb.ap()[:, g1 * 512:(g1 + 1) * 512], hscr.ap())
                        sc.copy(Pb.ap()[:, g1 * 512:(g1 + 1) * 512],
                                pscr.ap()).then_inc(sem["s_hsb"], 1)
                    k = tt - 13
                    if 0 <= k < KC and TRUNC >= 6:
                        wfk = wfr.ap()[:, k % 2, :]
                        if k % 2 == 0:
                            sc.wait_ge(sem["s_wf2"], 16 * (k // 2 + 1))
                        else:
                            sc.wait_ge(sem["s_wf1"], 128 + 16 * (k // 2 + 1))
                        sc.wait_ge(sem["s_delta"], 32)
                        if k >= 1:
                            sc.wait_ge(sem["s_m"], k)
                        sc.activation(sgn2.ap()[:, 0, :], wfk,
                                      Act.Sign, bias=deltaN.ap(), scale=1.0)
                        sc.activation(sgn2.ap()[:, 1, :], wfk,
                                      Act.Sign, bias=deltaP.ap(), scale=1.0)
                        sc.activation(mTb.ap()[:, k, :], wfk, Act.Relu,
                                      bias=deltaN.ap(), scale=1.0,
                                      accum_out=nmc.ap()[:, k:k + 1])
                        sc.activation(mTb.ap()[:, k, :], wfk, Act.Relu,
                                      bias=deltaN.ap(), scale=-1.0,
                                      accum_out=sqc.ap()[:, k:k + 1]).then_inc(sem["s_sg"], 1)
                    k2 = tt - 15
                    if 0 <= k2 < KC and TRUNC >= 6:
                        if k2 == 0:
                            sc.wait_ge(sem[f"s_xTp{7}"], 32)
                        sc.wait_ge(sem["s_m"], k2 + 1)
                        sc.activation(xnbq.ap()[:, 0, :], mTb.ap()[:, k2, :],
                                      Act.Abs, bias=0.0, scale=0.5,
                                      accum_out=awc.ap()[:, k2:k2 + 1]).then_inc(sem["s_sq2"], 1)
                if TRUNC >= 6:
                    sc.wait_ge(sem["s_svs"], 16)
                    sc.dma_start(out=sinvc.ap(),
                                 in_=scrB.ap()[1:2].rearrange(
                                     "(a b) -> a b", a=1)
                                 .broadcast_to([128, 1])).then_inc(sem["s_sinv"], 16)
                    sc.wait_ge(sem["s_cbs"], 16)
                    sc.dma_start(out=cbc.ap(),
                                 in_=scrB.ap()[2:3].rearrange("(a b) -> a b", a=1)
                                 .broadcast_to([128, 1])).then_inc(sem["s_cb"], 16)



            @blk.vector
            def _(v):
                mzc = [1]

                def mzp(instr):
                    instr.then_inc(sem["s_mz"], 1)
                    mzc[0] += 1

                def mzf():
                    v.wait_ge(sem["s_mz"], mzc[0])

                v.memset(magicc.ap(), MAGIC)
                v.memset(epsc.ap(), 1e-5)
                v.memset(z840.ap(), 0.0)
                v.memset(onesb.ap(), 1.0).then_inc(sem["s_mz"], 1)
                for g in range(4 if TRUNC >= 1 else 0):
                    v.wait_ge(sem[f"s_xg{g}"], 64)
                    for t in range(4 * g, 4 * g + 4):
                        ins = v.reduce_sum(sumc.ap()[:, t:t + 1],
                                           xbuf.ap()[:, t, :],
                                           AX.X)
                        if t == 4 * g + 3:
                            mzp(ins)
                    sl = slice(4 * g, 4 * g + 4)
                    v.wait_ge(sem["s_ssq"], 4 * g + 4)
                    mzf()
                    mzp(v.tensor_scalar(muc.ap()[:, sl], sumc.ap()[:, sl], RD,
                                        None, Alu.mult))
                    v.tensor_scalar(tmpc.ap()[:, sl], ssc.ap()[:, sl], RD,
                                    None, Alu.mult)
                    mzf()
                    mzp(v.tensor_tensor(tmc2.ap()[:, sl], muc.ap()[:, sl],
                                        muc.ap()[:, sl], Alu.mult))
                    mzf()
                    v.tensor_tensor(tmpc.ap()[:, sl], tmpc.ap()[:, sl],
                                    tmc2.ap()[:, sl], Alu.subtract).then_inc(sem["s_std"], 1)
                    v.wait_ge(sem["s_sts"], g + 1)
                    mzp(v.reciprocal(invc.ap()[:, sl], invc.ap()[:, sl]))
                    mzf()
                    mzp(v.tensor_tensor(tmc2.ap()[:, sl], tmpc.ap()[:, sl],
                                        invc.ap()[:, sl], Alu.mult))
                    mzf()
                    v.tensor_tensor(tmc2.ap()[:, sl], tmc2.ap()[:, sl],
                                    invc.ap()[:, sl], Alu.mult).then_inc(sem["s_std2"], 1)
                    v.wait_ge(sem["s_sts2"], g + 1)
                    mzp(v.reciprocal(c2c.ap()[:, sl], c2c.ap()[:, sl]))
                    mzf()
                    mzp(v.tensor_tensor(c2c.ap()[:, sl], invc.ap()[:, sl],
                                        c2c.ap()[:, sl], Alu.mult))
                    v.scalar_tensor_tensor(nm1c.ap()[:, sl], muc.ap()[:, sl],
                                           -1.0, invc.ap()[:, sl], Alu.mult,
                                           Alu.mult)
                    mzf()
                    mzp(v.scalar_tensor_tensor(nm2c.ap()[:, sl],
                                               muc.ap()[:, sl],
                                               -1.0, c2c.ap()[:, sl], Alu.mult,
                                               Alu.mult))
                    mzf()
                    v.sem_inc(sem["s_stats"], 1)
                    for t in range(4 * g, 4 * g + 4):
                        if t >= 2 and TRUNC >= 2:
                            v.wait_ge(sem[f"s_xTp{(t - 2) // 2}"], 32)
                        v.tensor_scalar(xnbq.ap()[:, t % 2, :],
                                        xbuf.ap()[:, t, :],
                                        invc.ap()[:, t:t + 1],
                                        nm1c.ap()[:, t:t + 1], Alu.mult,
                                        Alu.add).then_inc(sem["s_xnb"], 1)
                if TRUNC >= 3:
                    v.wait_ge(sem["s_aw"], KC)
                    v.reduce_sum(col1.ap(), awc.ap(), AX.X).then_inc(sem["s_dcol"], 1)
                for g in range(4 if TRUNC >= 3 else 0):
                    if g == 1:
                        v.wait_ge(sem["s_b4"], 16)
                        mzp(v.reduce_sum(sc11.ap()[:, 11:12], row128.ap(),
                                         AX.X))
                        mzf()
                        v.tensor_scalar(sc11.ap()[:, 4:5], sc11.ap()[:, 11:12],
                                        float(0.7 / (D * D)), None, Alu.mult)
                        mzp(v.tensor_scalar(sc11.ap()[:, 5:6],
                                            sc11.ap()[:, 11:12],
                                            float(-0.7 / (D * D)), None,
                                            Alu.mult))
                        mzf()
                        v.sem_inc(sem["s_b5"], 1)
                    for rt in range(4 * g, 4 * g + 4):
                        v.wait_ge(sem["s_xn2"], rt + 1)
                        v.reduce_max(lmax.ap()[:, rt:rt + 1],
                                     xn2r.ap()[:, rt % 2, :], AX.X,
                                     apply_absolute_value=True).then_inc(sem["s_lm"], 1)
                        st = rt - 2
                        if st >= 0 and TRUNC >= 4 and T4GP:
                            v.wait_ge(sem["s_p2"], st + 1)
                            v.tensor_tensor(xbuf.ap()[:, st, :],
                                            xbuf.ap()[:, st, :],
                                            gmr.ap()[:, st % 2, :],
                                            Alu.add).then_inc(sem["s_s1"], 1)
                        if rt >= 13 and TRUNC >= 6:
                            c = rt - 13
                            v.wait_ge(sem["s_sg"], c + 1)
                            v.tensor_tensor(mTb.ap()[:, c, :],
                                            sgn2.ap()[:, 0, :],
                                            sgn2.ap()[:, 1, :],
                                            Alu.add).then_inc(sem["s_m"], 1)
                    if TRUNC < 4 or not T4AB:
                        continue
                    sl2 = slice(g * 512, (g + 1) * 512)
                    v.wait_ge(sem["s_abe"], g + 1)
                    if g >= 1:
                        v.wait_ge(sem["s_hsb"], g)
                    ih = 0.0 if g == 0 else hscr.ap()[:, 511:512]
                    ip = 1.0 if g == 0 else pscr.ap()[:, 511:512]
                    v.tensor_tensor_scan(hscr.ap(), a_bf.ap()[:, sl2],
                                         b_bf.ap()[:, sl2], ih, Alu.mult,
                                         Alu.add)
                    v.tensor_tensor_scan(pscr.ap(), a_bf.ap()[:, sl2],
                                         a_bf.ap()[:, sl2], ip, Alu.mult,
                                         Alu.bypass).then_inc(sem["s_scan"], 1)
                for st in ((NT - 2, NT - 1) if TRUNC >= 4 and T4GP else ()):
                    v.wait_ge(sem["s_p2"], st + 1)
                    v.tensor_tensor(xbuf.ap()[:, st, :], xbuf.ap()[:, st, :],
                                    gmr.ap()[:, st % 2, :],
                                    Alu.add).then_inc(sem["s_s1"], 1)
                if TRUNC >= 6:
                    v.wait_ge(sem["s_lm"], 16)
                    v.reduce_max(lmax1.ap(), lmax.ap(), AX.X).then_inc(sem["s_b1"], 1)
                    v.wait_ge(sem["s_b2"], 16)
                    v.reduce_max(sc11.ap()[:, 0:1], row128.ap(), AX.X).then_inc(sem["s_b3"], 1)
                    for c in range(3, KC):
                        v.wait_ge(sem["s_sg"], c + 1)
                        v.tensor_tensor(mTb.ap()[:, c, :], sgn2.ap()[:, 0, :],
                                        sgn2.ap()[:, 1, :],
                                        Alu.add).then_inc(sem["s_m"], 1)
                if TRUNC >= 5:
                    v.wait_ge(sem["s_sum"], 1)
                    v.tensor_scalar(wchb.ap(), wcTb.ap(), hinc.ap(), None,
                                    Alu.mult).then_inc(sem["s_wch"], 1)
                def sinv_chain():
                    v.wait_ge(sem["s_gm2"], 1)
                    mzp(v.reciprocal(sc11.ap()[:, 1:2], sc11.ap()[:, 0:1]))
                    mzf()
                    v.tensor_scalar(sc11.ap()[:, 2:3], sc11.ap()[:, 1:2], 127.0,
                                    None, Alu.mult)
                    mzp(v.tensor_scalar(sc11.ap()[:, 3:4], sc11.ap()[:, 0:1],
                                        float(np.float32(1.0)
                                              / np.float32(127.0)),
                                        None, Alu.mult))
                    mzf()
                    v.sem_inc(sem["s_sv1"], 1)

                def alpha_part1():
                    v.wait_ge(sem["s_sg"], KC)
                    mzp(v.tensor_tensor(nmc.ap(), nmc.ap(), sqc.ap(), Alu.add))
                    mzf()
                    v.reduce_sum(col1.ap(), nmc.ap(), AX.X).then_inc(sem["s_acol"], 1)

                def alpha_part2():
                    v.wait_ge(sem["s_b6"], 16)
                    v.reduce_sum(sc11.ap()[:, 4:5], row128.ap(), AX.X)
                    v.wait_ge(sem["s_sq2"], KC)
                    v.reduce_sum(col1.ap(), awc.ap(), AX.X).then_inc(sem["s_acol"], 2)

                def alpha_part3():
                    v.wait_ge(sem["s_b6"], 32)
                    mzp(v.reduce_sum(sc11.ap()[:, 5:6], row128.ap(), AX.X))
                    mzf()
                    mzp(v.tensor_tensor(sc11.ap()[:, 6:7], sc11.ap()[:, 5:6],
                                        deltaP.ap()[0:1, 0:1], Alu.mult))
                    mzf()
                    mzp(v.tensor_tensor(sc11.ap()[:, 6:7], sc11.ap()[:, 6:7],
                                        sc11.ap()[:, 4:5], Alu.add))
                    mzf()
                    mzp(v.tensor_scalar_max(sc11.ap()[:, 7:8],
                                            sc11.ap()[:, 5:6], 1.0))
                    mzf()
                    mzp(v.reciprocal(sc11.ap()[:, 7:8], sc11.ap()[:, 7:8]))
                    mzf()
                    mzp(v.tensor_tensor(sc11.ap()[:, 8:9], sc11.ap()[:, 6:7],
                                        sc11.ap()[:, 7:8], Alu.mult))
                    mzf()
                    mzp(v.tensor_tensor(sc11.ap()[:, 9:10], sc11.ap()[:, 8:9],
                                        sc11.ap()[:, 3:4], Alu.mult))
                    mzf()
                    mzp(v.tensor_scalar(sc11.ap()[:, 10:11], sc11.ap()[:, 9:10],
                                        0.5, None, Alu.mult))
                    mzf()
                    v.sem_inc(sem["s_alpha"], 1)

                def round_q(rt):
                    if rt < 2:
                        v.wait_ge(sem[f"s_xTp{7}"], 32)
                        v.wait_ge(sem["s_sq2"], KC)
                    v.wait_ge(sem[f"s_rlk{rt // 2}"], 16)
                    v.wait_ge(sem["s_sinv"], 16)
                    if rt >= 2:
                        v.wait_ge(sem[f"s_qTp{(rt - 2) // 2}"], 32)
                    v.tensor_scalar(xn2r.ap()[:, rt % 2, :],
                                    xn2r.ap()[:, rt % 2, :], sinvc.ap(),
                                    magicc.ap(), Alu.mult, Alu.add)
                    v.tensor_scalar(xnbq.ap()[:, rt % 2, :],
                                    xn2r.ap()[:, rt % 2, :], -MAGIC, None,
                                    Alu.add).then_inc(sem["s_q"], 1)

                def out_stt(ot):
                    v.wait_ge(sem["s_ffn"], ot + 1)
                    v.wait_ge(sem["s_cb"], 16)
                    if DEBUG_DUMPS:
                        v.wait_ge(sem["s_dbgB"], 16 * NT)
                    par = [zgA, zgB][ot % 2]
                    v.scalar_tensor_tensor(xbuf.ap()[:, ot, :], par.ap(),
                                           cbc.ap(), xbuf.ap()[:, ot, :],
                                           Alu.mult, Alu.add).then_inc(sem["s_out"], 1)

                for tt in range(NT if TRUNC >= 5 else 0):
                    for h in (0, 1):
                        p = 2 * tt + h
                        ubuf = uP.ap() if p % 2 == 0 else dxP.ap()[:, 0:512]
                        v.wait_ge(sem["s_u"], p + 1)
                        v.wait_ge(sem["s_g"], tt + 1)
                        v.scalar_tensor_tensor(
                            gview(tt)[:, h * 512:(h + 1) * 512], ubuf, 1.0,
                            gview(tt)[:, h * 512:(h + 1) * 512], Alu.mult,
                            Alu.mult).then_inc(sem["s_p1"], 1)
                    v.wait_ge(sem["s_s1"], tt + 1)
                    if DEBUG_DUMPS:
                        v.wait_ge(sem["s_dbgA"], 16 * NT)
                    v.tensor_tensor(xbuf.ap()[:, tt, :], xbuf.ap()[:, tt, :],
                                    gview(tt), Alu.add).then_inc(sem["s_s2"], 1)
                    if TRUNC < 6:
                        continue
                    if tt == 2:
                        alpha_part1()
                    if tt == 3:
                        sinv_chain()
                    if tt == 4:
                        alpha_part2()
                    if tt == 6:
                        alpha_part3()
                    rt = tt - 5
                    if rt >= 0:
                        round_q(rt)
                    ot = tt - 7
                    if ot >= 0 and TRUNC >= 6:
                        out_stt(ot)
                if TRUNC >= 6:
                    for rt in range(NT - 5, NT):
                        round_q(rt)
                for tt in range(9 if TRUNC >= 6 else 0, NT if TRUNC >= 6 else 0):
                    out_stt(tt)

            @blk.gpsimd
            def _(gp):
                nio = [0]

                def gdma(out, in_):
                    gp.dma_start(out=out, in_=in_).then_inc(sem["s_io"], 16)
                    nio[0] += 16
                    gp.wait_ge(sem["s_io"], nio[0])

                for kc in range(KC):
                    gp.dma_start(out=wabTb.ap()[:, kc, :],
                                 in_=wabT_d[kc * 128:(kc + 1) * 128, :]).then_inc(sem["s_w"], 16)
                gp.dma_start(out=wcTb.ap(), in_=wcT_d).then_inc(sem["s_w"], 16)
                gp.dma_start(out=bDb.ap(), in_=bD_d).then_inc(sem["s_w"], 16)
                gp.dma_start(out=bgb.ap(), in_=bg_d).then_inc(sem["s_w"], 16)
                gp.dma_start(out=myoff_s.ap(), in_=myoff_d).then_inc(sem["s_w"], 16)
                gp.dma_start(out=partoff_s.ap(), in_=partoff_d).then_inc(sem["s_w"], 16)
                for kc in range(KC):
                    gp.dma_start(out=wgTb.ap()[:, kc, :],
                                 in_=wgT_d[kc * 128:(kc + 1) * 128, :]).then_inc(sem["s_w"], 16)
                    gp.dma_start(out=wdTb.ap()[:, kc, :],
                                 in_=wdT_d[kc * 128:(kc + 1) * 128, :]).then_inc(sem["s_w"], 16)
                for k in range(KC if TRUNC >= 1 else 0):
                    if k >= 1:
                        gp.wait_ge(sem["s_aw"], k)
                    gp.dma_start(out=wfr.ap()[:, 0, :],
                                 in_=wfT_d[k * 128:(k + 1) * 128, :]).then_inc(sem["s_wf1"], 16)
                moff = None
                if TRUNC >= 5:
                    gp.wait_ge(sem["s_mz"], 1)
                    gp.wait_ge(sem["s_w"], 16 * 29)
                    gdma(cc1_in.ap(), z840.ap())
                    moff = gp.value_load(myoff_s.ap())
                for tt in range(NT if TRUNC >= 3 else 0):
                    pt = tt - 2
                    if 0 <= pt and TRUNC >= 4 and T4GP:
                        gp.wait_ge(sem["s_gm"], pt + 1)
                        gp.wait_ge(sem["s_dxe"], pt + 1)
                        gp.tensor_tensor(gmr.ap()[:, pt % 2, :],
                                         gmr.ap()[:, pt % 2, :],
                                         dxe.ap()[:, pt % 2, :],
                                         Alu.mult).then_inc(sem["s_p2"], 1)
                    gp.wait_ge(sem["s_z"], tt + 1)
                    gp.wait_ge(sem["s_gb"], 32)
                    gp.tensor_tensor(xn2r.ap()[:, tt % 2, :],
                                     xn2r.ap()[:, tt % 2, :], gammaB.ap(),
                                     Alu.mult)
                    gp.tensor_tensor(xn2r.ap()[:, tt % 2, :],
                                     xn2r.ap()[:, tt % 2, :], betaB.ap(),
                                     Alu.add).then_inc(sem["s_xn2"], 1)
                    if tt == 2:
                        gp.wait_ge(sem["s_dcol"], 1)
                        gdma(scrA.ap().rearrange("(p one) -> p one", one=1),
                             col1.ap())
                        gp.dma_start(out=row128.ap(),
                                     in_=scrA.ap().rearrange(
                                         "(one p) -> one p", one=1)).then_inc(sem["s_b4"], 16)
                    if tt == 4:
                        gp.wait_ge(sem["s_b5"], 1)
                        gp.dma_start(out=scrB.ap()[4:6].rearrange("(a b) -> a b", a=1),
                                     in_=sc11.ap()[:, 4:6]).then_inc(sem["s_dscr"], 16)
                    if 12 <= tt and TRUNC >= 6:
                        k = tt - 12
                        gp.wait_ge(sem["s_aw"], KC)
                        if k >= 2:
                            gp.wait_ge(sem["s_sg"], k - 1)
                        gp.dma_start(out=wfr.ap()[:, k % 2, :],
                                     in_=wfT_d[k * 128:(k + 1) * 128, :]).then_inc(
                                         sem["s_wf2" if k % 2 == 0 else "s_wf1"], 16)
                if TRUNC >= 5:
                    gp.wait_ge(sem["s_scan"], 4)
                    gdma(scrH.ap().rearrange("(p one) -> p one", one=1),
                         hscr.ap()[:, 511:512])
                    gdma(cc1_in.ap().rearrange("r c -> (r c)")[bass.ds(moff, 16)],
                         scrH.ap())
                    gp.collective_compute(
                        "AllReduce", Alu.add,
                        replica_groups=[list(range(NCORE))],
                        ins=[cc1_in.ap().opt()],
                        outs=[cc1_out.ap().opt()]).then_inc(sem["s_cc1"], 1)
                for pt in ((NT - 2, NT - 1) if TRUNC >= 4 and T4GP else ()):
                    gp.wait_ge(sem["s_gm"], pt + 1)
                    gp.wait_ge(sem["s_dxe"], pt + 1)
                    gp.tensor_tensor(gmr.ap()[:, pt % 2, :],
                                     gmr.ap()[:, pt % 2, :],
                                     dxe.ap()[:, pt % 2, :],
                                     Alu.mult).then_inc(sem["s_p2"], 1)
                if TRUNC >= 6:
                    gp.wait_ge(sem["s_b1"], 1)
                    gp.wait_ge(sem["s_b5"], 1)
                    gdma(scrA.ap().rearrange("(p one) -> p one", one=1), lmax1.ap())
                    gp.dma_start(out=row128.ap(),
                                 in_=scrA.ap().rearrange("(one p) -> one p",
                                                         one=1)).then_inc(sem["s_b2"], 16)
                    gp.wait_ge(sem["s_b3"], 1)
                    gp.dma_start(out=scrB.ap()[0:1].rearrange("(a b) -> a b", a=1),
                                 in_=sc11.ap()[:, 0:1]).then_inc(sem["s_c2s"], 16)
                    gp.wait_ge(sem["s_c2b"], 16)
                    gp.collective_compute(
                        "AllReduce", Alu.max,
                        replica_groups=[list(range(NCORE))],
                        ins=[cc2_in.ap().opt()],
                        outs=[cc2_out.ap().opt()]).then_inc(sem["s_cc2"], 1)
                if TRUNC >= 6:
                    for k in range(4, KC):
                        gp.wait_ge(sem["s_sg"], k - 1)
                        gp.dma_start(out=wfr.ap()[:, k % 2, :],
                                     in_=wfT_d[k * 128:(k + 1) * 128, :]).then_inc(
                                         sem["s_wf2" if k % 2 == 0 else "s_wf1"], 16)
                if TRUNC >= 6:
                    for j in range(KC):
                        gp.wait_ge(sem[f"s_spk{j}"], 16)
                    gp.dma_start(
                        out=xn2r.ap()[:, 0:2, :],
                        in_=xn2_d[0:256, :].rearrange(
                            "(s p) d -> p s d", s=2)).then_inc(sem["s_rlk0"], 16)
                if TRUNC >= 5:
                    gp.wait_ge(sem["s_cc1"], 1)
                    poff = gp.value_load(partoff_s.ap())
                    gdma(scrH2.ap(),
                         cc1_out.ap().rearrange("r c -> (r c)")[bass.ds(poff, 16)])
                    gdma(hinc.ap(), scrH2.ap().rearrange("(p one) -> p one", one=1))
                    gp.sem_inc(sem["s_sum"], 1)
                if TRUNC >= 6:
                    gp.wait_ge(sem["s_cc2"], 1)
                    gdma(gath2.ap(), cc2_out.ap())
                    gp.tensor_scalar_max(sc11.ap()[:, 0:1], gath2.ap()[0:1, 0:1],
                                         1e-8).then_inc(sem["s_gm2"], 1)
                    gp.wait_ge(sem["s_sv1"], 1)
                    gp.dma_start(out=scrB.ap()[1:2].rearrange("(a b) -> a b", a=1),
                                 in_=sc11.ap()[:, 2:3]).then_inc(sem["s_svs"], 16)
                    gp.wait_ge(sem["s_acol"], 1)
                    gdma(scrA.ap().rearrange("(p one) -> p one", one=1), col1.ap())
                    gp.dma_start(out=row128.ap(),
                                 in_=scrA.ap().rearrange("(one p) -> one p",
                                                         one=1)).then_inc(sem["s_b6"], 16)
                    gp.wait_ge(sem["s_acol"], 3)
                    gdma(scrA.ap().rearrange("(p one) -> p one", one=1), col1.ap())
                    gp.dma_start(out=row128.ap(),
                                 in_=scrA.ap().rearrange("(one p) -> one p",
                                                         one=1)).then_inc(sem["s_b6"], 16)
                    gp.wait_ge(sem["s_alpha"], 1)
                    gp.dma_start(out=scrB.ap()[2:3].rearrange("(a b) -> a b", a=1),
                                 in_=sc11.ap()[:, 10:11]).then_inc(sem["s_cbs"], 16)
                    for k in range(1, KC):
                        gp.wait_ge(sem["s_q"], 2 * k)
                        gp.dma_start(
                            out=xn2r.ap()[:, 0:2, :],
                            in_=xn2_d[k * 256:(k + 1) * 256, :].rearrange(
                                "(s p) d -> p s d", s=2)).then_inc(sem[f"s_rlk{k}"], 16)

            @blk.tensor
            def _(t):
                if TRUNC >= 4:
                    t.wait_ge(sem["s_w"], 16 * 29)
                    t.wait_ge(sem["s_mz"], 1)
                for tt in range(NT if TRUNC >= 4 else 0):
                    g = tt // 4
                    par = [zgA, zgB][tt % 2]
                    t.wait_ge(sem[f"s_xTp{tt // 2}"], 32)
                    if tt >= 2 and T4G and T4G_G:
                        t.wait_ge(sem["s_g"], tt - 1)
                    if tt >= 2 and T4G and T4G_M:
                        t.wait_ge(sem["s_gm"], tt - 1)
                    if tt >= 1 and T4G and T4G_D:
                        t.wait_ge(sem["s_dxe"], tt)
                    for kc in range(KC):
                        lhs = xnTq.ap()[:, tt, kc, :]
                        for f in (0, 1):
                            fsl = slice(f * 512, (f + 1) * 512)
                            t.matmul(par.ap()[:, fsl], lhs,
                                     wgTb.ap()[:, kc, fsl],
                                     start=(kc == 0), stop=False)
                        for f in (0, 1):
                            fsl = slice(f * 512, (f + 1) * 512)
                            t.matmul(dxP.ap()[:, fsl], lhs,
                                     wdTb.ap()[:, kc, fsl],
                                     start=(kc == 0), stop=False)
                    for f in (0, 1):
                        fsl = slice(f * 512, (f + 1) * 512)
                        mm = t.matmul(par.ap()[:, fsl], onesb.ap(),
                                      bgb.ap()[:, fsl], start=False, stop=True)
                        if f == 1:
                            mm.then_inc(sem["s_zg"], 1)
                    for f in (0, 1):
                        fsl = slice(f * 512, (f + 1) * 512)
                        mm = t.matmul(dxP.ap()[:, fsl], onesb.ap(),
                                      bDb.ap()[:, fsl], start=False, stop=True)
                        if f == 1:
                            mm.then_inc(sem["s_dx"], 1)
                    if tt % 4 == 1:
                        t.wait_ge(sem[f"s_xTp{2 * g + 1}"], 32)
                        if g >= 1 and T4AB:
                            t.wait_ge(sem["s_abe"], g)
                        for kc in range(KC):
                            mm = t.matmul(abP.ap(), wabTb.ap()[:, kc, :],
                                          xnTq.ap()[:, 4 * g:4 * g + 4, kc, :],
                                          start=(kc == 0), stop=(kc == KC - 1))
                            if kc == KC - 1:
                                mm.then_inc(sem["s_ab"], 1)
                if TRUNC >= 5:
                    t.wait_ge(sem["s_wch"], 1)
                    t.wait_ge(sem["s_hsb"], 4)
                    t.wait_ge(sem["s_dxe"], NT)
                def ffn_tile(et):
                    par = [zgA, zgB][et % 2]
                    if et == 0:
                        t.wait_ge(sem["s_m"], KC)
                    t.wait_ge(sem[f"s_qTp{et // 2}"], 32)
                    if et < 2:
                        t.wait_ge(sem["s_g"], NT)
                        t.wait_ge(sem["s_gm"], NT)
                    else:
                        t.wait_ge(sem["s_out"], et - 1)
                    for kc in range(KC):
                        lhs = xnTq.ap()[:, et, kc, :]
                        for f in (0, 1):
                            fsl = slice(f * 512, (f + 1) * 512)
                            mm = t.matmul(par.ap()[:, fsl], lhs,
                                          mTb.ap()[:, kc, fsl],
                                          start=(kc == 0), stop=(kc == KC - 1))
                            if kc == KC - 1 and f == 1:
                                mm.then_inc(sem["s_ffn"], 1)

                for ut in range(NT if TRUNC >= 5 else 0):
                    for h in (0, 1):
                        fsl = slice(h * 512, (h + 1) * 512)
                        p = 2 * ut + h
                        ubuf = uP.ap() if p % 2 == 0 else dxP.ap()[:, 0:512]
                        if p >= 2:
                            t.wait_ge(sem["s_p1"], p - 1)
                        t.matmul(ubuf,
                                 hsb.ap()[:, ut * 128:(ut + 1) * 128],
                                 wcTb.ap()[:, fsl], start=True, stop=False)
                        t.matmul(ubuf,
                                 Pb.ap()[:, ut * 128:(ut + 1) * 128],
                                 wchb.ap()[:, fsl], start=False,
                                 stop=True).then_inc(sem["s_u"], 1)
                    if ut >= 7 and TRUNC >= 6:
                        ffn_tile(ut - 7)
                for et in range(9 if TRUNC >= 6 else 0, NT if TRUNC >= 6 else 0):
                    ffn_tile(et)

    return nc



_CACHED = {}


def kernel(L0_attn_x, L0_attn_WA, L0_attn_WB, L0_attn_WC, L0_attn_WD,
           L0_attn_bD, L0_attn_Wg, L0_attn_bg, L0_ffn_weight,
           L0_ffn_gamma, L0_ffn_beta):
    x = np.ascontiguousarray(np.asarray(L0_attn_x, np.float32))
    B, Sq, D_ = x.shape
    wgT = np.ascontiguousarray(np.asarray(L0_attn_Wg, np.float32).T)
    wdT = np.ascontiguousarray(np.asarray(L0_attn_WD, np.float32).T)
    wfT = np.ascontiguousarray(np.asarray(L0_ffn_weight, np.float32).T)
    wabT = np.ascontiguousarray(
        np.concatenate([np.asarray(L0_attn_WA, np.float32),
                        np.zeros((16, D), np.float32),
                        np.asarray(L0_attn_WB, np.float32)], axis=0).T)
    wcT = np.ascontiguousarray(np.asarray(L0_attn_WC, np.float32).T)
    gamma = np.ascontiguousarray(np.asarray(L0_ffn_gamma, np.float32).reshape(1, D))
    beta = np.ascontiguousarray(np.asarray(L0_ffn_beta, np.float32).reshape(1, D))
    bD = np.ascontiguousarray(np.asarray(L0_attn_bD, np.float32).reshape(1, D))
    bg = np.ascontiguousarray(np.asarray(L0_attn_bg, np.float32).reshape(1, D))

    if "nc" not in _CACHED:
        _CACHED["nc"] = build_nc()
    nc = _CACHED["nc"]

    in_maps = []
    for c in range(NCORE):
        b, h = c // 2, c % 2
        in_maps.append(dict(
            x=np.ascontiguousarray(x[b, h * TC:(h + 1) * TC, :]),
            wgT=wgT, wdT=wdT, wfT=wfT, wabT=wabT, wcT=wcT,
            gamma=gamma, beta=beta, bD=bD, bg=bg,
            myoff=np.array([[c * 40 + 2]], dtype=np.int32),
            partoff=np.array([[(c - 1) * 40 + 2 if h == 1 else 20]],
                             dtype=np.int32),
        ))
    res = run_bass_kernel_spmd(nc, in_maps, core_ids=list(range(NCORE)))
    out = np.empty((B, Sq, D), np.float32)
    for c in range(NCORE):
        b, h = c // 2, c % 2
        out[b, h * TC:(h + 1) * TC, :] = res.results[c]["out"]
    return out
